# version 14
# baseline (speedup 1.0000x reference)
import contextlib
import numpy as np

import concourse.bass as bass
import concourse.mybir as mybir
from concourse.bass_utils import run_bass_kernel_spmd

F32 = mybir.dt.float32
BF16 = mybir.dt.bfloat16
I32 = mybir.dt.int32
ALU = mybir.AluOpType
AF = mybir.ActivationFunctionType
AX = mybir.AxisListType

D = 4096
HC = 32
OCC = 24
T = 1024
NT = 8
NEG = -1e9
EPS_LN = 1e-5
EPS_PRIOR = 1e-12

ENGS = ("sync", "gpsimd", "tensor", "vector", "scalar")

_STOP = None
_P3S = None
_ACCUM = True


def build_program():
    nc = bass.Bass("TRN2", target_bir_lowering=False)

    def din(name, shape, dtype=F32):
        return nc.dram_tensor(name, shape, dtype, kind="ExternalInput").ap()

    x_d = din("x", [T, D])
    xh_d = din("xh", [4, D])
    cw_d = din("cw", [OCC, 128, 12288])
    cb_d = din("cb", [128, OCC])
    wq_d = din("wq", [128, 8192])
    wk_d = din("wk", [128, 8192])
    wv_d = din("wv", [128, 8192])
    wo_d = din("wo", [128, 8192])
    bq_d = din("bq", [128, 8])
    bk_d = din("bk", [128, 8])
    bo_d = din("bo", [128, 8])
    bv_d = din("bv", [128, 8])
    pw_d = din("pw", [128, 32768])
    pb_d = din("pb", [1, 4096])
    wb_d = din("wb", [HC, 128, 4096])
    bb_d = din("bb", [128, 32])
    la_d = din("la", [128, 4096])
    lb_d = din("lb", [128, 4096])
    rw_d = din("rw", [128, 512])
    rb_d = din("rb", [16, 1])
    pr_d = din("pr", [16, 1])
    ac_d = din("ac", [1, 4], I32)

    out_h = nc.dram_tensor("out", [T, D], F32, kind="ExternalOutput")
    out_d = out_h.ap()

    es = contextlib.ExitStack()
    scopes = {}

    def sbuf(name, shape, dtype, side="left"):
        s = contextlib.ExitStack()
        scopes[name] = s
        return s.enter_context(nc.sbuf_tensor(name, shape, dtype, side=side))

    def psum(name, shape, dtype=F32):
        s = contextlib.ExitStack()
        scopes[name] = s
        return s.enter_context(nc.psum_tensor(name, shape, dtype))

    def free(*names):
        for n in names:
            scopes.pop(n).close()

    class Sem:
        def __init__(self, name):
            self.h = es.enter_context(nc.semaphore(name))
            self.n = 0

    dS = Sem("dS")
    dX = [Sem("dXa"), Sem("dXb")]
    dWa = Sem("dWa")
    dWb = Sem("dWb")
    dWc = Sem("dWc")
    dW2 = Sem("dW2")
    dW2b = Sem("dW2b")
    dW3 = Sem("dW3")
    dWq = [Sem("dWq0"), Sem("dWq1")]
    dX5 = [Sem("dX5a"), Sem("dX5b")]
    dW4 = [Sem("dW4a"), Sem("dW4b"), Sem("dW4c"), Sem("dW4d")]
    dO = [Sem("dOa"), Sem("dOb")]
    dA = [Sem("dAa"), Sem("dAb")]
    tP = Sem("tP"); vP = Sem("vP"); sP = Sem("sP"); gP = Sem("gP")

    prog = {e: [] for e in ENGS}

    def em(eng, fn):
        prog[eng].append(fn)

    def wait(eng, sem, val):
        em(eng, lambda e, h=sem.h, v=val: e.wait_ge(h, v))

    CHAIN = {"vector": vP, "scalar": sP, "gpsimd": gP}
    chain_val = {"vector": 0, "scalar": 0, "gpsimd": 0}

    def run(eng, fn, sem=None, k=1, chain=True):
        if chain and eng in CHAIN and chain_val[eng] > 0:
            wait(eng, CHAIN[eng], chain_val[eng])
        if sem is None:
            em(eng, fn)
            return None
        sem.n += k
        em(eng, lambda e, f=fn, h=sem.h, kk=k: f(e).then_inc(h, kk))
        if eng in CHAIN and sem is CHAIN[eng]:
            chain_val[eng] = sem.n
        return sem.n

    ident_b = sbuf("ident_b", [128, 128], BF16)
    ident_f = sbuf("ident_f", [128, 128], F32)
    onesT = sbuf("onesT", [128, 128], BF16)
    ones_f = sbuf("ones_f", [1, 128], F32)
    cst = sbuf("cst", [128, 4], F32)
    bq_s = sbuf("bq_s", [128, 8], F32)
    bk_s = sbuf("bk_s", [128, 8], F32)
    bo_s = sbuf("bo_s", [128, 8], F32)
    bv_s = sbuf("bv_s", [128, 8], F32)
    cb_s = sbuf("cb_s", [128, OCC], F32)
    bb_s = sbuf("bb_s", [128, 32], F32)
    st2 = sbuf("st2", [128, 4], F32)
    rst = sbuf("rst", [128, 1], F32)
    LBP = sbuf("LBP", [128, 4096], BF16)
    ctile = sbuf("ctile", [128, 4], F32)
    exp16 = sbuf("exp16", [16, 128], F32)
    sel16 = sbuf("sel16", [4, 128], F32)
    ohb = sbuf("ohb", [16, 4], BF16)
    ohf = sbuf("ohf", [16, 4], F32)
    iot = sbuf("iot", [16, 1], F32)
    actf = sbuf("actf", [1, 4], F32)
    acts = sbuf("acts", [1, 4], I32)
    pr_s = sbuf("pr_s", [16, 1], F32)
    rb_s = sbuf("rb_s", [16, 1], F32)
    c16 = sbuf("c16", [16, 1], F32)
    csel = sbuf("csel", [1, 4], F32)
    trash2 = sbuf("trash2", [128, D], BF16)

    u1fm = sbuf("u1fm", [128, HC * 1028], BF16)
    cwq = [sbuf("cwq0", [128, 3072], BF16), sbuf("cwq1", [128, 3072], BF16)]
    xf = [sbuf("xf0", [128, D], F32), sbuf("xf1", [128, D], F32)]
    u1t = [sbuf("u1t0", [128, D], BF16), sbuf("u1t1", [128, D], BF16)]
    trash = sbuf("trash", [128, D], BF16)
    ht = sbuf("ht", [4, D], F32)
    hst = sbuf("hst", [4, 4], F32)
    hrst = sbuf("hrst", [4, 1], F32)
    u1h = sbuf("u1h", [4, D], BF16)

    psT = [psum("psT0", [128, 1024], BF16), psum("psT1", [128, 1024], BF16)]
    psS = psum("psS", [128, 512], F32)

    for dst, src in ((ht, xh_d), (acts, ac_d), (pr_s, pr_d), (rb_s, rb_d),
                     (bq_s, bq_d), (bk_s, bk_d), (bo_s, bo_d), (bv_s, bv_d),
                     (cb_s, cb_d), (bb_s, bb_d)):
        run("sync", lambda q, a=dst, b=src: q.dma_start(out=a[:, :], in_=b),
            dS, 16)
    m_in_setup = dS.n

    run("gpsimd", lambda g: g.memset(ident_b[:, :], 1.0), gP)
    run("gpsimd", lambda g: g.affine_select(
        ident_b[:, :], ident_b[:, :], [[-1, 128]], ALU.is_equal,
        0.0, base=0, channel_multiplier=1), gP)
    run("gpsimd", lambda g: g.memset(ident_f[:, :], 1.0), gP)
    run("gpsimd", lambda g: g.affine_select(
        ident_f[:, :], ident_f[:, :], [[-1, 128]], ALU.is_equal,
        0.0, base=0, channel_multiplier=1), gP)
    def g_c1(g):
        g.memset(onesT[:, :], 1.0)
        g.memset(ones_f[:, :], 1.0)
        g.memset(cst[:, 0:1], EPS_LN)
        g.memset(cst[:, 1:2], -0.5)
        g.memset(cst[:, 2:3], EPS_PRIOR)
        g.memset(cst[:, 3:4], 1.0 / D)
        g.iota(iot[:, :], [[0, 1]], base=0, channel_multiplier=1,
               allow_small_or_imprecise_dtypes=True)
        return g.memset(exp16[:, :], 1.0)
    run("gpsimd", g_c1, gP)
    run("gpsimd", lambda g: g.affine_select(
        exp16[:, :], exp16[:, :], [[1, 128]], ALU.is_ge, 0.0,
        base=0, channel_multiplier=-8), gP)
    m_g_const = run("gpsimd", lambda g: g.affine_select(
        exp16[:, :], exp16[:, :], [[-1, 128]], ALU.is_ge, 0.0,
        base=7, channel_multiplier=8), gP)

    run("gpsimd", lambda g: g.dma_start(LBP[32:33, 0:4096], pb_d), dWb, 16)
    m_pb = dWb.n

    def ln_stats(xin, stats, rstd, nparts, vwait=None):
        ms = run("scalar", lambda s: s.activation(
            trash[0:nparts, :], xin, AF.Square, accum_out=stats[:, 1:2]), sP)
        if vwait is not None:
            wait("vector", vwait[0], vwait[1])
        run("vector", lambda v, nparts=nparts: v.tensor_scalar(
            trash2[0:nparts, :], xin, 1.0, 0.0, ALU.mult, ALU.add,
            accum_out=stats[:, 0:1]), vP)
        wait("vector", sP, ms)
        m1 = run("vector", lambda v: v.tensor_scalar_mul(
            stats[:, 2:3], stats[:, 0:1], 1.0 / D), vP)
        wait("vector", vP, m1)
        m2 = run("vector", lambda v: v.tensor_scalar_mul(
            stats[:, 0:1], stats[:, 1:2], 1.0 / D), vP)
        wait("vector", vP, m2)
        m3 = run("vector", lambda v: v.tensor_tensor(
            stats[:, 1:2], stats[:, 2:3], stats[:, 2:3], ALU.mult), vP)
        wait("vector", vP, m3)
        m4 = run("vector", lambda v: v.tensor_sub(
            stats[:, 3:4], stats[:, 0:1], stats[:, 1:2]), vP)
        wait("scalar", vP, m4)
        run("scalar", lambda s: s.activation(
            rstd[:, :], stats[0:nparts, 3:4], AF.Ln,
            bias=cst[0:nparts, 0:1]), sP)
        return run("scalar", lambda s: s.activation(
            rstd[:, :], rstd[:, :], AF.Exp, scale=cst[0:nparts, 1:2]), sP)

    norm_m = {}
    tev_m = {}
    xin_m = {}

    ms0_m = {}
    vs0_m = {}
    vs5_m = {}

    def emit_ln0a(t):
        wait("scalar", dX[t % 2], xin_m[t])
        ms0_m[t] = run("scalar", lambda s, t=t: s.activation(
            trash[:, :], xf[t % 2][:, :], AF.Square,
            accum_out=st2[:, 1:2]), sP)
        wait("vector", dX[t % 2], xin_m[t])
        vs0_m[t] = run("vector", lambda v, t=t: v.tensor_scalar(
            trash2[:, :], xf[t % 2][:, :], 1.0, 0.0, ALU.mult, ALU.add,
            accum_out=st2[:, 0:1]), vP)

    def emit_ln0b(t):
        wait("vector", sP, ms0_m[t])
        wait("scalar", vP, vs0_m[t])
        run("scalar", lambda s: s.activation(
            st2[:, 2:3], st2[:, 0:1], AF.Identity, scale=cst[:, 3:4]), sP)
        run("scalar", lambda s: s.activation(
            st2[:, 0:1], st2[:, 2:3], AF.Square), sP)
        mq = run("scalar", lambda s: s.activation(
            st2[:, 1:2], st2[:, 1:2], AF.Identity, scale=cst[:, 3:4],
            bias=cst[:, 0:1]), sP)
        wait("vector", sP, mq)
        m4 = run("vector", lambda v: v.tensor_sub(
            st2[:, 3:4], st2[:, 1:2], st2[:, 0:1]), vP)
        wait("scalar", vP, m4)
        run("scalar", lambda s: s.activation(
            rst[:, :], st2[:, 3:4], AF.Ln), sP)
        mr = run("scalar", lambda s: s.activation(
            rst[:, :], rst[:, :], AF.Exp, scale=cst[:, 1:2]), sP)
        wait("vector", sP, mr)
        norm_m[t] = run("vector", lambda v, t=t: v.tensor_scalar(
            u1t[t % 2][:, :], xf[t % 2][:, :], st2[:, 2:3], rst[:, 0:1],
            ALU.subtract, ALU.mult), vP)

    def emit_ln0(t):
        emit_ln0a(t)
        emit_ln0b(t)

    def emit_xdma(t):
        if t >= 2:
            wait("sync", vP, norm_m[t - 2])
        run("sync", lambda q, t=t: q.dma_start(
            out=xf[t % 2][:, :], in_=x_d[t * 128:(t + 1) * 128, :]),
            dX[t % 2], 16)
        xin_m[t] = dX[t % 2].n

    wait("scalar", gP, m_g_const)
    wait("vector", gP, m_g_const)
    gidx = 0
    for t in range(NT):
        if t == 0:
            emit_xdma(0)
            emit_xdma(1)
            emit_ln0(0)
        if t + 2 < NT:
            emit_xdma(t + 2)

        scol = (t // 4) * 514 + 1 + (t % 4) * 128
        for grp in range(4):
            wait("tensor", vP, norm_m[t])
            if gidx >= 2:
                wait("tensor", vP, tev_m[gidx - 2])
            def t_grp(tt, t=t, grp=grp, par=gidx % 2):
                last = None
                for j in range(8):
                    hcc = grp * 8 + j
                    last = tt.transpose(psT[par][:, j * 128:(j + 1) * 128],
                                        u1t[t % 2][:, hcc * 128:(hcc + 1) * 128],
                                        ident_b[:, :])
                return last
            mT = run("tensor", t_grp, tP)
            wait("vector", tP, mT)
            def v_grp(v, grp=grp, scol=scol, par=gidx % 2):
                o = bass.AP(u1fm, (grp * 8) * 1028 + scol,
                            [[HC * 1028, 128], [1028, 8], [1, 128]])
                i = bass.AP(psT[par], 0, [[1024, 128], [128, 8], [1, 128]])
                return v.tensor_copy(o, i)
            tev_m[gidx] = run("vector", v_grp, vP)
            gidx += 1
            if grp == 2 and t + 1 < NT:
                emit_ln0a(t + 1)
            if grp == 3 and t + 1 < NT:
                emit_ln0b(t + 1)

    wait("scalar", dS, m_in_setup)
    mr = ln_stats(ht[:, :], hst, hrst, 4, vwait=(dS, m_in_setup))
    wait("vector", sP, mr)
    m_u1h = run("vector", lambda v: v.tensor_scalar(
        u1h[:, :], ht[:, :], hst[:, 2:3], hrst[:, 0:1],
        ALU.subtract, ALU.mult), vP)
    wait("tensor", vP, m_u1h)
    wait("tensor", gP, m_g_const)
    wait("tensor", vP, tev_m[gidx - 1])
    wait("tensor", vP, tev_m[gidx - 2])
    def t_halo(t):
        last = None
        for hcc in range(HC):
            last = t.transpose(psT[0][:, hcc * 4: hcc * 4 + 4],
                               u1h[:, hcc * 128:(hcc + 1) * 128],
                               ident_b[0:4, 0:4])
        return last
    m_haloT = run("tensor", t_halo, tP)
    wait("vector", tP, m_haloT)
    def v_halo_evac(v):
        o0 = bass.AP(u1fm, 0, [[HC * 1028, 128], [1028, HC], [513, 2]])
        i0 = bass.AP(psT[0], 0, [[1024, 128], [4, HC], [1, 2]])
        v.tensor_copy(o0, i0)
        o1 = bass.AP(u1fm, 514, [[HC * 1028, 128], [1028, HC], [513, 2]])
        i1 = bass.AP(psT[0], 2, [[1024, 128], [4, HC], [1, 2]])
        return v.tensor_copy(o1, i1)
    m_halo_ev = run("vector", v_halo_evac, vP)

    wait("vector", dS, m_in_setup)
    m_actf = run("vector", lambda v: v.tensor_copy(actf[:, :], acts[:, :]), vP)
    wait("tensor", vP, m_actf)
    wait("tensor", gP, m_g_const)
    m_ohp = run("tensor", lambda t: t.matmul(
        psS[0:16, 0:4], ones_f[0:1, 0:16], actf[:, :], start=True, stop=True),
        tP)
    wait("vector", tP, m_ohp)
    run("vector", lambda v: v.tensor_tensor(
        ohf[:, :], psS[0:16, 0:4], iot[:, :].broadcast_to([16, 4]),
        ALU.is_equal), vP)
    m_oh = run("vector", lambda v: v.tensor_copy(ohb[:, :], ohf[:, :]), vP)

    wait("scalar", dS, m_in_setup)
    wait("scalar", gP, m_g_const)
    m_lnp = run("scalar", lambda s: s.activation(
        c16[:, :], pr_s[:, :], AF.Ln, bias=cst[0:16, 2:3]), sP)
    wait("vector", sP, m_lnp)
    m_c16 = run("vector", lambda v: v.tensor_add(
        c16[:, :], c16[:, :], rb_s[:, :]), vP)
    wait("tensor", vP, m_c16)
    m_cs = run("tensor", lambda t: t.matmul(
        psS[0:1, 4:8], c16[:, :], ohf[:, :], start=True, stop=True), tP)
    wait("vector", tP, m_cs)
    m_cse = run("vector", lambda v: v.tensor_copy(csel[:, :], psS[0:1, 4:8]),
                vP)
    wait("tensor", vP, m_cse)
    m_ct = run("tensor", lambda t: t.matmul(
        psS[:, 8:12], ones_f[:, :], csel[:, :], start=True, stop=True), tP)
    wait("vector", tP, m_ct)
    m_ctile = run("vector", lambda v: v.tensor_copy(ctile[:, :], psS[:, 8:12]),
                  vP)

    wait("tensor", vP, m_oh)
    wait("tensor", vP, m_ctile)
    m_s16 = run("tensor", lambda t: t.matmul(
        psS[0:4, 0:128], ohf[:, :], exp16[:, :], start=True, stop=True), tP)
    wait("vector", tP, m_s16)
    m_sel16 = run("vector", lambda v: v.tensor_copy(
        sel16[:, :], psS[0:4, 0:128]), vP)

    m_p0_done_v = vP.n
    m_p0_done_t = tP.n

    def _finish_early(sem, val):
        wait("sync", sem, val)
        run("sync", lambda q: q.dma_start(out_d[0:128, 0:128],
                                          ident_f[:, 0:128]), dO[0], 16)
        wait("sync", dO[0], dO[0].n)
        block = es.enter_context(nc.Block())
        for eng in ENGS:
            def _runner(e, fns=tuple(prog[eng])):
                for f in fns:
                    f(e)
            getattr(block, eng)(_runner)
        for name in reversed(list(scopes)):
            scopes.pop(name).close()
        es.close()
        return nc

    if _STOP == "P0":
        return _finish_early(vP, m_p0_done_v)

    free("u1h", "hrst", "hst", "ht",
         "trash", "u1t1", "u1t0", "xf1", "xf0")

    qkvfm = sbuf("qkvfm", [128, OCC * 1024], BF16, side="right")
    wq_s = sbuf("wq_s", [128, 8192], BF16, side="right")
    wk_s = sbuf("wk_s", [128, 8192], BF16, side="right")
    psC = [psum("psC0", [128, 1024], F32), psum("psC1", [128, 1024], F32)]

    qdma_m = {}
    qte_m = {}
    cev_m = {}
    for oc in range(OCC):
        for q in range(4):
            k = oc * 4 + q
            if k in (48, 52, 56, 60):
                if k == 48:
                    wait("gpsimd", tP, m_p0_done_t)
                i4 = (k - 48) // 4
                wsrc = (wq_d, wq_d, wk_d, wk_d)[i4]
                wdst = (wq_s, wq_s, wk_s, wk_s)[i4]
                hlf = (i4 % 2) * 4096
                run("gpsimd", lambda g, wsrc=wsrc, wdst=wdst, hlf=hlf:
                    g.dma_start(wdst[:, hlf:hlf + 4096],
                                wsrc[:, hlf:hlf + 4096]), dW2, 16)
            if k >= 2:
                wait("gpsimd", tP, qte_m[k - 2])
            run("gpsimd", lambda g, oc=oc, q=q, k=k: g.dma_start(
                cwq[k % 2][:, :], cw_d[oc, :, q * 3072:(q + 1) * 3072]),
                dWq[k % 2], 16)
            qdma_m[k] = dWq[k % 2].n
            wait("tensor", dWq[k % 2], qdma_m[k])
            if q == 0:
                wait("tensor", vP, cev_m[oc - 2] if oc >= 2 else m_p0_done_v)
            def t_conv(t, q=q, k=k, oc=oc):
                last = None
                for gi in range(24):
                    g = q * 24 + gi
                    w, hcc = g // HC, g % HC
                    for seg in range(2):
                        base = hcc * 1028 + seg * 514 + w
                        last = t.matmul(
                            psC[oc % 2][:, seg * 512:(seg + 1) * 512],
                            cwq[k % 2][:, gi * 128:(gi + 1) * 128],
                            u1fm[:, base: base + 512],
                            start=(g == 0), stop=(g == 95))
                return last
            qte_m[k] = run("tensor", t_conv, tP)
        wait("vector", tP, qte_m[oc * 4 + 3])
        cev_m[oc] = run("vector", lambda v, oc=oc: v.tensor_scalar_add(
            qkvfm[:, oc * 1024:(oc + 1) * 1024], psC[oc % 2][:, :],
            cb_s[:, oc:oc + 1]), vP)

    m_conv_t = tP.n
    m_conv_v = vP.n
    if _STOP == "P1":
        return _finish_early(vP, m_conv_v)

    free("cwq1", "cwq0", "u1fm")
    wo_s = sbuf("wo_s", [128, 8192], BF16)
    qh_s = sbuf("qh_s", [128, 8192], BF16)
    kh_s = sbuf("kh_s", [128, 8192], BF16)
    vh_s = sbuf("vh_s", [128, 8192], BF16)
    o_fm = sbuf("o_fm", [128, 8192], BF16)
    wv_s = sbuf("wv_s", [128, 8192], BF16)

    wait("gpsimd", tP, m_conv_t)
    run("gpsimd", lambda g: g.dma_start(wv_s[:, :], wv_d), dW2b, 16)
    run("gpsimd", lambda g: g.dma_start(wo_s[:, :], wo_d), dW2b, 16)
    m_wq = m_wk = 64
    m_wv = m_wo = 32

    p2ev = {}
    p2i = 0
    for kind in range(3):
        wm = (m_wq, m_wk, m_wv)[kind]
        ws = (dW2, dW2, dW2b)[kind]
        for m in range(8):
            wait("tensor", ws, wm)
            wait("tensor", vP, p2ev[p2i - 2] if p2i >= 2 else m_conv_v)
            def t_p2(t, kind=kind, m=m, par=p2i % 2):
                last = None
                wsrc = (wq_s, wk_s, wv_s)[kind]
                for k in range(8):
                    for hf in range(2):
                        if kind < 2:
                            lhs = wsrc[:, (k * 8 + m) * 128:
                                       (k * 8 + m + 1) * 128]
                            rhs = qkvfm[:, (kind * 8 + k) * 1024 + hf * 512:
                                        (kind * 8 + k) * 1024 + (hf + 1) * 512]
                        else:
                            lhs = qkvfm[:, (16 + k) * 1024 + m * 128:
                                        (16 + k) * 1024 + (m + 1) * 128]
                            rhs = wsrc[:, k * 1024 + hf * 512:
                                       k * 1024 + (hf + 1) * 512]
                        last = t.matmul(
                            psC[par][:, hf * 512:(hf + 1) * 512],
                            lhs, rhs, start=(k == 0), stop=(k == 7))
                return last
            mte = run("tensor", t_p2, tP)
            wait("vector", tP, mte)
            if kind == 0:
                fn = (lambda v, m=m, par=p2i % 2: v.tensor_scalar(
                    qh_s[:, m * 1024:(m + 1) * 1024], psC[par][:, :],
                    bq_s[:, m:m + 1], 0.125, ALU.add, ALU.mult))
            elif kind == 1:
                fn = (lambda v, m=m, par=p2i % 2: v.tensor_scalar_add(
                    kh_s[:, m * 1024:(m + 1) * 1024], psC[par][:, :],
                    bk_s[:, m:m + 1]))
            else:
                fn = (lambda v, m=m, par=p2i % 2: v.tensor_copy(
                    vh_s[:, m * 1024:(m + 1) * 1024], psC[par][:, :]))
            p2ev[p2i] = run("vector", fn, vP)
            p2i += 1
    m_p2_t = tP.n
    m_p2_v = vP.n
    if _STOP == "P2":
        return _finish_early(vP, m_p2_v)

    free("wv_s")
    free("wk_s", "wq_s", "qkvfm")
    free("psC1", "psC0", "psS", "psT1", "psT0")
    mask16 = sbuf("mask16", [128, 2048], F32)
    sm_s = sbuf("sm_s", [128, 2048], F32)
    ss_s = sbuf("ss_s", [128, 2048], F32)
    p32_s = sbuf("p32_s", [128, 2048], F32)
    pbf_s = sbuf("pbf_s", [128, 2048], BF16)
    pT_s = sbuf("pT_s", [128, 2048], BF16)
    nm16 = sbuf("nm16", [128, 16], F32)
    rs16 = sbuf("rs16", [128, 16], F32)
    ri16 = sbuf("ri16", [128, 16], F32)
    pw_s = sbuf("pw_s", [128, 32768], BF16, side="right")

    psA = psum("psA", [128, 2048], F32)
    psPT = psum("psPT", [128, 2048], BF16)
    psO = psum("psO", [128, 1024], F32)

    wait("gpsimd", tP, m_p2_t)
    run("gpsimd", lambda g: g.memset(mask16[:, :], 0.0), gP)
    m_mask = run("gpsimd", lambda g: g.affine_select(
        mask16[:, :], mask16[:, :], [[0, 16], [-1, 128]],
        ALU.is_ge, NEG, base=0, channel_multiplier=1), gP)
    for i in range(4):
        run("gpsimd", lambda g, i=i: g.dma_start(
            pw_s[:, i * 8192:(i + 1) * 8192],
            pw_d[:, i * 8192:(i + 1) * 8192]), dW3, 16)
    m_pw = dW3.n

    smv = {}; pbfv = {}; ptev = {}; oev = {}; expv = {}
    ptm = {}; pvm = {}; scm = {}

    def emit_sc(b):
        if b == 0:
            wait("tensor", vP, m_p2_v)
        else:
            wait("tensor", vP, smv[(b - 1, 1)])
        def t_sc(t, b=b):
            last = None
            for h in list(range(0, 16, 2)) + list(range(1, 16, 2)):
                r0 = 64 * (h % 2)
                cc = (h // 2) * 1024 + b * 128
                last = t.matmul(psA[:, h * 128:(h + 1) * 128],
                                qh_s[r0:r0 + 64, cc:cc + 128],
                                kh_s[r0:r0 + 64, cc:cc + 128],
                                start=True, stop=True)
            return last
        scm[b] = run("tensor", t_sc, tP)

    for b in range(NT):
        if b == 0:
            emit_sc(0)
        m_sc = scm[b]

        for half in range(2):
            c0 = half * 1024
            h0 = half * 8
            wait("vector", tP, m_sc)
            if b == 0 and half == 0:
                wait("vector", gP, m_mask)
            if b > 0:
                wait("vector", sP, expv[(b - 1, half)])
            smv[(b, half)] = run("vector", lambda v, c0=c0: v.tensor_add(
                sm_s[:, c0:c0 + 1024], psA[:, c0:c0 + 1024],
                mask16[:, c0:c0 + 1024]), vP)
            run("vector", lambda v, c0=c0, h0=h0: v.reduce_max(
                nm16[:, h0:h0 + 8],
                sm_s[:, c0:c0 + 1024].rearrange("p (a b) -> p a b", b=128),
                axis=AX.X, negate=True), vP)
            nmm = vP.n

            wait("scalar", vP, nmm)
            if b > 0:
                wait("scalar", vP, pbfv[(b - 1, half)])
            def s_exp(s, h0=h0):
                last = None
                for h in range(h0, h0 + 8):
                    last = s.activation(
                        p32_s[:, h * 128:(h + 1) * 128],
                        sm_s[:, h * 128:(h + 1) * 128], AF.Exp,
                        bias=nm16[:, h:h + 1], accum_out=rs16[:, h:h + 1])
                return last
            expv[(b, half)] = run("scalar", s_exp, sP)

        for half in range(2):
            c0 = half * 1024
            h0 = half * 8
            wait("vector", sP, expv[(b, half)])
            run("vector", lambda v, h0=h0: v.reciprocal(
                ri16[:, h0:h0 + 8], rs16[:, h0:h0 + 8]), vP)
            m_rc = vP.n
            wait("vector", vP, m_rc)
            if b > 0:
                wait("vector", tP, ptm[(b - 1, half)])
            pbfv[(b, half)] = run("vector", lambda v, c0=c0, h0=h0: v.tensor_tensor(
                pbf_s[:, c0:c0 + 1024].rearrange("p (a b) -> p a b", b=128),
                p32_s[:, c0:c0 + 1024].rearrange("p (a b) -> p a b", b=128),
                ri16[:, h0:h0 + 8].unsqueeze(2).broadcast_to([128, 8, 128]),
                ALU.mult), vP)

        for half in range(2):
            c0 = half * 1024
            h0 = half * 8
            if half == 1 and b + 1 < NT:
                emit_sc(b + 1)
            wait("tensor", vP, pbfv[(b, half)])
            if b > 0:
                wait("tensor", sP, ptev[(b - 1, half)])
            def t_pt(t, h0=h0):
                last = None
                for h in range(h0, h0 + 8):
                    last = t.transpose(psPT[:, h * 128:(h + 1) * 128],
                                       pbf_s[:, h * 128:(h + 1) * 128],
                                       ident_b[:, :])
                return last
            ptm[(b, half)] = run("tensor", t_pt, tP)

            wait("scalar", tP, ptm[(b, half)])
            if b > 0:
                wait("scalar", tP, pvm[(b - 1, half)])
            ptev[(b, half)] = run("scalar", lambda s, c0=c0: s.copy(
                pT_s[:, c0:c0 + 1024], psPT[:, c0:c0 + 1024]), sP)

            wait("tensor", sP, ptev[(b, half)])
            if b > 0:
                wait("tensor", vP, oev[(b - 1, half)])
            def t_pv(t, b=b, h0=h0):
                last = None
                for h in range(h0, h0 + 8):
                    r0 = 64 * (h % 2)
                    cc = (h // 2) * 128
                    last = t.matmul(psO[r0:r0 + 64, cc:cc + 128],
                                    vh_s[:, b * 1024 + h * 64:
                                         b * 1024 + (h + 1) * 64],
                                    pT_s[:, h * 128:(h + 1) * 128],
                                    start=True, stop=True,
                                    tile_position=(0, r0))
                return last
            pvm[(b, half)] = run("tensor", t_pv, tP)

            wait("vector", tP, pvm[(b, half)])
            def v_oe(v, b=b, half=half):
                last = None
                for m in range(half * 4, half * 4 + 4):
                    last = v.tensor_scalar_add(
                        o_fm[:, m * 1024 + b * 128: m * 1024 + (b + 1) * 128],
                        psO[:, m * 128:(m + 1) * 128], bv_s[:, m:m + 1])
                return last
            oev[(b, half)] = run("vector", v_oe, vP)
    m_p3_t = tP.n
    m_p3_v = vP.n
    if _STOP == "P3":
        return _finish_early(vP, vP.n)

    free("ri16", "rs16", "nm16", "pT_s", "pbf_s", "p32_s", "ss_s", "sm_s",
         "mask16")
    free("psO", "psPT", "psA")
    o2_s = sbuf("o2_s", [128, 8192], BF16, side="right")
    psC2 = [psum("psC2a", [128, 1024], F32), psum("psC2b", [128, 1024], F32)]

    p4ev = {}
    for m in range(8):
        wait("tensor", dW2b, m_wo)
        wait("tensor", vP, p4ev[m - 2] if m >= 2 else m_p3_v)
        def t_p4(t, m=m):
            last = None
            for k in range(8):
                for hf in range(2):
                    last = t.matmul(
                        psC2[m % 2][:, hf * 512:(hf + 1) * 512],
                        wo_s[:, (k * 8 + m) * 128:(k * 8 + m + 1) * 128],
                        o_fm[:, k * 1024 + hf * 512: k * 1024 + (hf + 1) * 512],
                        start=(k == 0), stop=(k == 7))
            return last
        mte = run("tensor", t_p4, tP)
        wait("vector", tP, mte)
        p4ev[m] = run("vector", lambda v, m=m: v.tensor_scalar_add(
            o2_s[:, m * 1024:(m + 1) * 1024], psC2[m % 2][:, :],
            bo_s[:, m:m + 1]), vP)
    m_p4_t = tP.n
    m_p4_v = vP.n
    if _STOP == "P4":
        return _finish_early(vP, m_p4_v)

    free("wv_s") if False else None
    free("o_fm", "vh_s", "kh_s", "qh_s", "wo_s")
    free("psC2b", "psC2a")
    u2tm = sbuf("u2tm", [128, 32768], BF16)
    x2 = [sbuf("x2a", [128, D], F32), sbuf("x2b", [128, D], F32)]
    xq = [sbuf("xqa", [128, 1024], F32), sbuf("xqb", [128, 1024], F32)]
    psP = psum("psP", [128, 1024], F32)

    xq_m = {}
    tt_m = {}
    st5_m = {}
    nrm5_m = {}
    x2dma_m = {}
    def emit_xq(t, qb):
        if t == 0 and qb == 0:
            wait("gpsimd", tP, m_p4_t)
        if qb >= 2:
            wait("gpsimd", vP, tt_m[(t, (qb - 2) * 2 + 1)])
        elif t >= 1:
            wait("gpsimd", vP, tt_m[(t - 1, (qb + 2) * 2 + 1)])
        run("gpsimd", lambda g, t=t, qb=qb: g.dma_start(
            xq[qb % 2][:, :],
            x_d[t * 128:(t + 1) * 128, qb * 1024:(qb + 1) * 1024]),
            dX5[qb % 2], 16)
        xq_m[(t, qb)] = dX5[qb % 2].n

    st5a_m = {}

    def emit_ln5a(t):
        wait("scalar", vP, tt_m[(t, 7)])
        st5_m[t] = run("scalar", lambda s, t=t: s.activation(
            u2tm[:, t * 4096:(t + 1) * 4096], x2[t % 2][:, :],
            AF.Square, accum_out=st2[:, 1:2]), sP)
        vs5_m[t] = run("vector", lambda v, t=t: v.tensor_scalar(
            trash2[:, :], x2[t % 2][:, :], 1.0, 0.0,
            ALU.mult, ALU.add, accum_out=st2[:, 0:1]), vP)

    def emit_ln5b(t):
        wait("vector", sP, st5_m[t])
        wait("scalar", vP, vs5_m[t])
        run("scalar", lambda s: s.activation(
            st2[:, 2:3], st2[:, 0:1], AF.Identity, scale=cst[:, 3:4]), sP)
        run("scalar", lambda s: s.activation(
            st2[:, 0:1], st2[:, 2:3], AF.Square), sP)
        mq = run("scalar", lambda s: s.activation(
            st2[:, 1:2], st2[:, 1:2], AF.Identity, scale=cst[:, 3:4],
            bias=cst[:, 0:1]), sP)
        wait("vector", sP, mq)
        m4 = run("vector", lambda v: v.tensor_sub(
            st2[:, 3:4], st2[:, 1:2], st2[:, 0:1]), vP)
        wait("scalar", vP, m4)
        run("scalar", lambda s: s.activation(
            rst[:, :], st2[:, 3:4], AF.Ln), sP)
        mr5 = run("scalar", lambda s: s.activation(
            rst[:, :], rst[:, :], AF.Exp, scale=cst[:, 1:2]), sP)
        wait("vector", sP, mr5)
        nrm5_m[t] = run("vector", lambda v, t=t: v.tensor_scalar(
            u2tm[:, t * 4096:(t + 1) * 4096], x2[t % 2][:, :],
            st2[:, 2:3], rst[:, 0:1], ALU.subtract, ALU.mult), vP)

    for t in range(NT):
        emit_xq(t, 0)
        emit_xq(t, 1)
        for bk in range(8):
            if bk == 4:
                emit_xq(t, 2)
                if t >= 1:
                    emit_ln5a(t - 1)
            if bk == 6 and t >= 1:
                emit_ln5b(t - 1)
            if bk == 6:
                emit_xq(t, 3)
            gi = t * 8 + bk
            wait("tensor", dW3, m_pw)
            wait("tensor", dWb, m_pb)
            if gi >= 2:
                wait("tensor", vP, tt_m[(t, bk - 2)] if bk >= 2
                     else tt_m[(t - 1, 6 + bk)])
            else:
                wait("tensor", vP, m_p4_v)
            if bk == 0 and t >= 2:
                wait("tensor", sP, st5_m[t - 2])
                wait("tensor", vP, nrm5_m[t - 2])
                wait("tensor", dO[t % 2], x2dma_m[t - 2])
            def t_p5(t_, t=t, bk=bk):
                par = bk % 2
                for r in range(8):
                    t_.matmul(psP[:, par * 512:(par + 1) * 512],
                              o2_s[:, r * 1024 + t * 128:
                                   r * 1024 + (t + 1) * 128],
                              pw_s[:, r * 4096 + bk * 512:
                                   r * 4096 + (bk + 1) * 512],
                              start=(r == 0), stop=False)
                return t_.matmul(psP[:, par * 512:(par + 1) * 512],
                                 onesT[32:33, 0:128],
                                 LBP[32:33, bk * 512:(bk + 1) * 512],
                                 start=False, stop=True)
            mte = run("tensor", t_p5, tP)
            wait("vector", tP, mte)
            wait("vector", dX5[(bk // 2) % 2], xq_m[(t, bk // 2)])
            if t >= 2 and bk == 0:
                wait("vector", sP, st5_m[t - 2])
                wait("vector", dO[t % 2], x2dma_m[t - 2])
            tt_m[(t, bk)] = run("vector", lambda v, t=t, bk=bk: v.tensor_tensor(
                x2[t % 2][:, bk * 512:(bk + 1) * 512],
                psP[:, (bk % 2) * 512:((bk % 2) + 1) * 512],
                xq[(bk // 2) % 2][:, (bk % 2) * 512:((bk % 2) + 1) * 512],
                ALU.add), vP)

        wait("sync", vP, tt_m[(t, 7)])
        run("sync", lambda q, t=t: q.dma_start(
            out_d[t * 128:(t + 1) * 128, :], x2[t % 2][:, :]), dO[t % 2], 16)
        x2dma_m[t] = dO[t % 2].n
    emit_ln5a(NT - 1)
    emit_ln5b(NT - 1)
    m_p5_t = tP.n
    if _STOP == "P5":
        return _finish_early(vP, nrm5_m[NT - 1])

    free("xqb", "xqa", "x2b", "x2a")
    free("o2_s", "pw_s")
    free("psP")
    u2fm = sbuf("u2fm", [128, 32768], BF16, side="right")
    la2 = sbuf("la2", [128, 4096], BF16, side="right")
    lball = sbuf("lball", [128, 4096], BF16, side="right")
    rwt = sbuf("rwt", [128, 512], BF16, side="right")
    wait("gpsimd", tP, m_p5_t)
    run("gpsimd", lambda g: g.dma_start(la2[:, :], la_d), dWa, 16)
    run("gpsimd", lambda g: g.dma_start(lball[:, :], lb_d), dWa, 16)
    run("gpsimd", lambda g: g.dma_start(rwt[:, :], rw_d), dWa, 16)
    m_lw2 = dWa.n
    psT2 = [psum("psT2a", [128, 1024], BF16), psum("psT2b", [128, 1024], BF16)]

    g2 = 0
    tev2 = {}
    for t in range(NT):
        for grp in range(4):
            wait("tensor", vP, nrm5_m[t])
            if g2 < 2:
                wait("tensor", vP, tt_m[(NT - 1, 7)])
            if g2 >= 2:
                wait("tensor", vP, tev2[g2 - 2])
            def t_g2(tt, t=t, grp=grp, par=g2 % 2):
                last = None
                for j in range(8):
                    hcc = grp * 8 + j
                    last = tt.transpose(psT2[par][:, j * 128:(j + 1) * 128],
                                        u2tm[:, t * 4096 + hcc * 128:
                                             t * 4096 + (hcc + 1) * 128],
                                        ident_b[:, :])
                return last
            mT = run("tensor", t_g2, tP)
            wait("vector", tP, mT)
            def v_g2(v, t=t, grp=grp, par=g2 % 2):
                o = bass.AP(u2fm, (grp * 8) * 1024 + t * 128,
                            [[32768, 128], [1024, 8], [1, 128]])
                i = bass.AP(psT2[par], 0, [[1024, 128], [128, 8], [1, 128]])
                return v.tensor_copy(o, i)
            tev2[g2] = run("vector", v_g2, vP)
            g2 += 1
    m_p5b_v = vP.n
    if _STOP == "P5b":
        return _finish_early(vP, m_p5b_v)

    free("u2tm")
    free("psT2b", "psT2a")
    lgf = sbuf("lgf", [4, 1024], F32)
    lga = sbuf("lga", [16, 1024], BF16)
    alf = sbuf("alf", [4, 1024], F32)
    alfb = sbuf("alfb", [4, 1024], BF16)
    zscB = sbuf("zscB", [128, 1024], BF16)
    ua4 = sbuf("ua4", [128, 4], F32)
    al4 = sbuf("al4", [128, 4], F32)
    nm4 = sbuf("nm4", [128, 1], F32)
    rc4 = sbuf("rc4", [128, 1], F32)
    ae_s = sbuf("ae_s", [128, 1024], F32)
    sel16b = sbuf("sel16b", [4, 128], BF16)
    psZ = psum("psZ", [128, 1024], F32)
    psL = psum("psL", [16, 1024], F32)
    psM = psum("psM", [128, 512], F32)
    psE = psum("psE", [128, 1024], F32)

    wait("tensor", vP, m_p5b_v)
    wait("tensor", dWa, m_lw2)
    def t_z(t):
        last = None
        for hcc in range(HC):
            for hf in range(2):
                last = t.matmul(psZ[:, hf * 512:(hf + 1) * 512],
                                la2[:, hcc * 128:(hcc + 1) * 128],
                                u2fm[:, hcc * 1024 + hf * 512:
                                     hcc * 1024 + (hf + 1) * 512],
                                start=(hcc == 0), stop=(hcc == HC - 1))
        return last
    m_z = run("tensor", t_z, tP)

    def t_lg(t):
        last = None
        for hcc in range(HC):
            for hf in range(2):
                last = t.matmul(psL[:, hf * 512:(hf + 1) * 512],
                                rwt[:, hcc * 16:(hcc + 1) * 16],
                                u2fm[:, hcc * 1024 + hf * 512:
                                     hcc * 1024 + (hf + 1) * 512],
                                start=(hcc == 0), stop=(hcc == HC - 1))
        return last
    m_lg = run("tensor", t_lg, tP)
    wait("vector", tP, m_lg)
    wait("vector", dO[0], 64)
    wait("vector", dO[1], 64)
    m_lga = run("vector", lambda v: v.tensor_copy(lga[:, :], psL[:, :]), vP)
    wait("vector", vP, m_lga)
    m_ohb2 = run("vector", lambda v: v.tensor_copy(sel16b[:, :], sel16[:, :]),
                 vP)
    wait("tensor", vP, m_ohb2)
    def t_sel(t):
        t.matmul(psM[0:4, 0:512], ohb[:, :], lga[:, 0:512],
                 start=True, stop=True)
        return t.matmul(psM[0:4, 0:512][0:4, 0:512] if False else
                        psL[0:4, 0:512], ohb[:, :], lga[:, 512:1024],
                        start=True, stop=True)
    m_self = run("tensor", t_sel, tP)
    wait("vector", tP, m_self)
    def v_self(v):
        v.tensor_copy(lgf[:, 0:512], psM[0:4, 0:512])
        return v.tensor_copy(lgf[:, 512:1024], psL[0:4, 0:512])
    m_lgf = run("vector", v_self, vP)

    alf_m = {}
    for t in range(NT):
        wait("tensor", vP, m_lgf if t == 0 else alf_m[t - 1])
        m_t1 = run("tensor", lambda tt, t=t: tt.transpose(
            psM[:, 0:4], lgf[:, t * 128:(t + 1) * 128], ident_f[0:4, 0:4]), tP)
        wait("vector", tP, m_t1)
        m_ua = run("vector", lambda v: v.tensor_add(
            ua4[:, :], psM[:, 0:4], ctile[:, :]), vP)
        wait("vector", vP, m_ua)
        m_n4 = run("vector", lambda v: v.reduce_max(
            nm4[:, :], ua4[:, :], axis=AX.X, negate=True), vP)
        wait("scalar", vP, m_n4)
        m_e4 = run("scalar", lambda s: s.activation(
            al4[:, :], ua4[:, :], AF.Exp, bias=nm4[:, 0:1]), sP)
        wait("vector", sP, m_e4)
        m_s4 = run("vector", lambda v: v.reduce_sum(
            nm4[:, :], al4[:, :], axis=AX.X), vP)
        wait("vector", vP, m_s4)
        m_r4 = run("vector", lambda v: v.reciprocal(rc4[:, :], nm4[:, :]), vP)
        wait("vector", vP, m_r4)
        m_a4 = run("vector", lambda v: v.tensor_scalar_mul(
            ua4[:, :], al4[:, :], rc4[:, 0:1]), vP)
        wait("tensor", vP, m_a4)
        m_t2 = run("tensor", lambda tt: tt.transpose(
            psM[0:4, 4:132], ua4[:, :], ident_f[:, :]), tP)
        wait("vector", tP, m_t2)
        alf_m[t] = run("vector", lambda v, t=t: v.tensor_copy(
            alfb[:, t * 128:(t + 1) * 128], psM[0:4, 4:132]), vP)

    wait("tensor", vP, alf_m[NT - 1])
    def t_ae(t):
        t.matmul(psE[:, 0:512], sel16b[:, :], alfb[:, 0:512],
                 start=True, stop=True)
        return t.matmul(psE[:, 512:1024], sel16b[:, :], alfb[:, 512:1024],
                        start=True, stop=True)
    m_ae = run("tensor", t_ae, tP)
    wait("vector", tP, m_ae)
    m_aes = run("vector", lambda v: v.tensor_copy(ae_s[:, :], psE[:, :]), vP)
    wait("vector", vP, m_aes)
    wait("vector", tP, m_z)
    m_zsc = run("vector", lambda v: v.scalar_tensor_tensor(
        zscB[:, :], psZ[:, :], 2.0, ae_s[:, :], ALU.mult, ALU.mult), vP)

    free("sel16b", "ae_s", "rc4", "nm4", "al4", "ua4", "zscB", "alfb", "alf", "lga", "lgf")
    wbb = [sbuf("wbb%d" % i, [128, 4096], BF16, side="right") for i in range(4)]
    ff_fm = [sbuf("ffa", [128, 1024], BF16), sbuf("ffb", [128, 1024], BF16)]
    ff_t = [sbuf("fta", [128, 1024], F32), sbuf("ftb", [128, 1024], F32)]

    wb_m = {}
    wait("gpsimd", tP, m_p5_t)
    wait("gpsimd", vP, nrm5_m[NT - 1])
    wait("gpsimd", dO[0], 64)
    wait("gpsimd", dO[1], 64)
    for oc in range(4):
        run("gpsimd", lambda g, oc=oc: g.dma_start(
            wbb[oc % 4][:, :], wb_d[oc]), dW4[oc % 4], 16)
        wb_m[oc] = dW4[oc % 4].n

    free("psE", "psM", "psL", "psZ")
    psF = [psum("psFa", [128, 1024], F32), psum("psFb", [128, 1024], F32)]
    psX = [psum("psXa", [128, 1024], BF16), psum("psXb", [128, 1024], BF16)]

    f_mm = {}; f_ev1 = {}; f_tt = {}; f_ev2 = {}; f_dma = {}
    for oc in range(32):
        wait("tensor", dW4[oc % 4], wb_m[oc])
        wait("tensor", vP, f_ev1[oc - 2] if oc >= 2 else m_zsc)
        if oc == 0:
            wait("tensor", dWb, m_pb)
            wait("vector", dO[0], 64)
            wait("vector", dO[1], 64)
        def t_f(t, oc=oc):
            par = oc % 2
            for hcc in range(HC):
                for hf in range(2):
                    t.matmul(psF[par][:, hf * 512:(hf + 1) * 512],
                             wbb[oc % 4][:, hcc * 128:(hcc + 1) * 128],
                             u2fm[:, hcc * 1024 + hf * 512:
                                  hcc * 1024 + (hf + 1) * 512],
                             start=(hcc == 0), stop=False)
            last = None
            for hf in range(2):
                last = t.matmul(psF[par][:, hf * 512:(hf + 1) * 512],
                                lball[:, oc * 128:(oc + 1) * 128],
                                zscB[:, hf * 512:(hf + 1) * 512],
                                start=False, stop=True)
            return last
        f_mm[oc] = run("tensor", t_f, tP)
        if oc + 4 < 32:
            wait("gpsimd", tP, f_mm[oc])
            run("gpsimd", lambda g, oc=oc: g.dma_start(
                wbb[(oc + 4) % 4][:, :], wb_d[oc + 4]), dW4[(oc + 4) % 4], 16)
            wb_m[oc + 4] = dW4[(oc + 4) % 4].n
        wait("vector", tP, f_mm[oc])
        f_ev1[oc] = run("vector", lambda v, oc=oc: v.tensor_scalar_add(
            ff_fm[oc % 2][:, :], psF[oc % 2][:, :], bb_s[:, oc:oc + 1]), vP)
        wait("tensor", vP, f_ev1[oc])
        if oc >= 2:
            wait("tensor", vP, f_ev2[oc - 2])
        def t_ft(t, oc=oc):
            last = None
            for j in range(8):
                last = t.transpose(psX[oc % 2][:, j * 128:(j + 1) * 128],
                                   ff_fm[oc % 2][:, j * 128:(j + 1) * 128],
                                   ident_b[:, :])
            return last
        f_tt[oc] = run("tensor", t_ft, tP)
        wait("vector", tP, f_tt[oc])
        if oc >= 2:
            wait("vector", dA[oc % 2], f_dma[oc - 2])
        f_ev2[oc] = run("vector", lambda v, oc=oc: v.tensor_copy(
            ff_t[oc % 2][:, :], psX[oc % 2][:, :]), vP)
        wait("gpsimd", vP, f_ev2[oc])
        if oc == 0:
            wait("gpsimd", dO[0], 64)
            wait("gpsimd", dO[1], 64)
        def g_acc(g, oc=oc):
            o = bass.AP(out_h, oc * 128,
                        [[4096, 128], [128 * 4096, 8], [1, 128]])
            if _ACCUM:
                return g.dma_start(o, ff_t[oc % 2][:, :], accum_op=ALU.add)
            return g.dma_start(o, ff_t[oc % 2][:, :])
        run("gpsimd", g_acc, dA[oc % 2], 16)
        f_dma[oc] = dA[oc % 2].n

    wait("gpsimd", dO[0], 64)
    wait("gpsimd", dO[1], 64)
    wait("gpsimd", dA[0], dA[0].n)
    wait("gpsimd", dA[1], dA[1].n)

    block = es.enter_context(nc.Block())
    for eng in ENGS:
        def _runner(e, fns=tuple(prog[eng])):
            for f in fns:
                f(e)
        getattr(block, eng)(_runner)

    for name in reversed(list(scopes)):
        scopes.pop(name).close()
    es.close()
    return nc


def _prep(inputs):
    f = np.ascontiguousarray
    x = inputs["L0_attn_inputs"]
    conv_w = inputs["L0_attn_conv_w"]
    conv_b = inputs["L0_attn_conv_b"]
    wq = inputs["L0_attn_wq"].reshape(1024, 1024)
    bq = inputs["L0_attn_bq"].reshape(1024)
    wk = inputs["L0_attn_wk"].reshape(1024, 1024)
    bk = inputs["L0_attn_bk"].reshape(1024)
    wv = inputs["L0_attn_wv"].reshape(1024, 1024)
    bv = inputs["L0_attn_bv"].reshape(1024)
    wo = inputs["L0_attn_wo"].reshape(1024, 1024)
    bo = inputs["L0_attn_bo"].reshape(1024)
    pw = inputs["L0_attn_proj_w"]
    pb = inputs["L0_attn_proj_b"]
    wb = inputs["L0_ffn_W_base"]
    bb = inputs["L0_ffn_b_base"]
    la = inputs["L0_ffn_lora_A"]
    lb = inputs["L0_ffn_lora_B"]
    rw = inputs["L0_ffn_router_W"]
    rb = inputs["L0_ffn_router_b"]
    pr = inputs["L0_ffn_domain_prior"]
    ac = inputs["L0_ffn_active"]

    com = {
        "cw": f(np.asarray(conv_w).reshape(3, 32, 128, 24, 128)
                .transpose(3, 2, 0, 1, 4).reshape(24, 128, 12288)),
        "cb": f(np.asarray(conv_b).reshape(24, 128).T),
        "wq": f(np.asarray(wq).reshape(8, 128, 8, 128)
                .transpose(1, 0, 2, 3).reshape(128, 8192)),
        "wk": f(np.asarray(wk).reshape(8, 128, 8, 128)
                .transpose(1, 0, 2, 3).reshape(128, 8192)),
        "wv": f(np.asarray(wv).reshape(8, 128, 1024)
                .transpose(1, 0, 2).reshape(128, 8192)),
        "wo": f(np.asarray(wo).reshape(8, 128, 8, 128)
                .transpose(1, 0, 2, 3).reshape(128, 8192)),
        "bq": f(np.asarray(bq).reshape(8, 128).T),
        "bk": f(np.asarray(bk).reshape(8, 128).T),
        "bo": f(np.asarray(bo).reshape(8, 128).T),
        "bv": f(np.asarray(bv).reshape(8, 128).T),
        "pw": f(np.asarray(pw).reshape(8, 128, 4096)
                .transpose(1, 0, 2).reshape(128, 32768)),
        "pb": f(np.asarray(pb).reshape(1, 4096)),
        "wb": f(np.asarray(wb).T.reshape(32, 128, 32, 128)
                .transpose(2, 1, 0, 3).reshape(32, 128, 4096)),
        "bb": f(np.asarray(bb).reshape(32, 128).T),
        "la": f(np.asarray(la).transpose(2, 0, 1).reshape(4096, 128)
                .reshape(32, 128, 128).transpose(1, 0, 2).reshape(128, 4096)),
        "lb": f(np.asarray(lb).transpose(0, 2, 1).reshape(128, 4096)),
        "rw": f(np.asarray(rw).T.reshape(32, 128, 16)
                .transpose(1, 0, 2).reshape(128, 512)),
        "rb": f(np.asarray(rb).reshape(16, 1)),
        "pr": f(np.asarray(pr).reshape(16, 1)),
        "ac": f(np.asarray(ac).reshape(1, 4)),
    }

    x = np.asarray(x)
    in_maps = []
    for c in range(8):
        lo, hi = c * 512, (c + 1) * 512
        xs = f(np.concatenate([x[0, lo:hi, :], x[1, lo:hi, :]], axis=0))
        halo = np.zeros((4, 4096), dtype=np.float32)
        if lo > 0:
            halo[0] = x[0, lo - 1, :]
            halo[2] = x[1, lo - 1, :]
        if hi < 4096:
            halo[1] = x[0, hi, :]
            halo[3] = x[1, hi, :]
        m = {"x": xs, "xh": f(halo)}
        m.update(com)
        in_maps.append(m)
    return in_maps


def kernel(**inputs):
    in_maps = _prep(inputs)
    nc = build_program()
    res = run_bass_kernel_spmd(nc, in_maps, core_ids=list(range(8)))
    out = np.empty((2, 4096, 4096), dtype=np.float32)
    for c in range(8):
        r = res.results[c]["out"]
        out[0, c * 512:(c + 1) * 512, :] = r[0:512]
        out[1, c * 512:(c + 1) * 512, :] = r[512:1024]
    return out


# version 15 (final)
# speedup vs baseline: 1.0135x; 1.0135x over previous
import contextlib
import numpy as np

import concourse.bass as bass
import concourse.mybir as mybir
from concourse.bass_utils import run_bass_kernel_spmd

F32 = mybir.dt.float32
BF16 = mybir.dt.bfloat16
I32 = mybir.dt.int32
ALU = mybir.AluOpType
AF = mybir.ActivationFunctionType
AX = mybir.AxisListType

D = 4096
HC = 32
OCC = 24
T = 1024
NT = 8
NEG = -1e9
EPS_LN = 1e-5
EPS_PRIOR = 1e-12

ENGS = ("sync", "gpsimd", "tensor", "vector", "scalar")

_STOP = None
_P3S = None
_ACCUM = True


def build_program():
    nc = bass.Bass("TRN2", target_bir_lowering=False)

    def din(name, shape, dtype=F32):
        return nc.dram_tensor(name, shape, dtype, kind="ExternalInput").ap()

    x_d = din("x", [T, D])
    xh_d = din("xh", [4, D])
    cw_d = din("cw", [OCC, 128, 12288])
    cb_d = din("cb", [128, OCC])
    wq_d = din("wq", [128, 8192])
    wk_d = din("wk", [128, 8192])
    wv_d = din("wv", [128, 8192])
    wo_d = din("wo", [128, 8192])
    bq_d = din("bq", [128, 8])
    bk_d = din("bk", [128, 8])
    bo_d = din("bo", [128, 8])
    bv_d = din("bv", [128, 8])
    pw_d = din("pw", [128, 32768])
    pb_d = din("pb", [1, 4096])
    wb_d = din("wb", [HC, 128, 4096])
    bb_d = din("bb", [128, 32])
    la_d = din("la", [128, 4096])
    lb_d = din("lb", [128, 4096])
    rw_d = din("rw", [128, 512])
    rb_d = din("rb", [16, 1])
    pr_d = din("pr", [16, 1])
    ac_d = din("ac", [1, 4], I32)

    out_h = nc.dram_tensor("out", [T, D], F32, kind="ExternalOutput")
    out_d = out_h.ap()

    es = contextlib.ExitStack()
    scopes = {}

    def sbuf(name, shape, dtype, side="left"):
        s = contextlib.ExitStack()
        scopes[name] = s
        return s.enter_context(nc.sbuf_tensor(name, shape, dtype, side=side))

    def psum(name, shape, dtype=F32):
        s = contextlib.ExitStack()
        scopes[name] = s
        return s.enter_context(nc.psum_tensor(name, shape, dtype))

    def free(*names):
        for n in names:
            scopes.pop(n).close()

    class Sem:
        def __init__(self, name):
            self.h = es.enter_context(nc.semaphore(name))
            self.n = 0

    dS = Sem("dS")
    dX = [Sem("dXa"), Sem("dXb")]
    dWa = Sem("dWa")
    dWb = Sem("dWb")
    dWc = Sem("dWc")
    dW2 = Sem("dW2")
    dW2b = Sem("dW2b")
    dW3 = Sem("dW3")
    dWq = [Sem("dWq0"), Sem("dWq1")]
    dX5 = [Sem("dX5a"), Sem("dX5b")]
    dW4 = [Sem("dW4a"), Sem("dW4b"), Sem("dW4c"), Sem("dW4d")]
    dO = [Sem("dOa"), Sem("dOb")]
    dA = [Sem("dAa"), Sem("dAb")]
    tP = Sem("tP"); vP = Sem("vP"); sP = Sem("sP"); gP = Sem("gP")

    prog = {e: [] for e in ENGS}

    def em(eng, fn):
        prog[eng].append(fn)

    def wait(eng, sem, val):
        em(eng, lambda e, h=sem.h, v=val: e.wait_ge(h, v))

    CHAIN = {"vector": vP, "scalar": sP, "gpsimd": gP}
    chain_val = {"vector": 0, "scalar": 0, "gpsimd": 0}

    def run(eng, fn, sem=None, k=1, chain=True):
        if chain and eng in CHAIN and chain_val[eng] > 0:
            wait(eng, CHAIN[eng], chain_val[eng])
        if sem is None:
            em(eng, fn)
            return None
        sem.n += k
        em(eng, lambda e, f=fn, h=sem.h, kk=k: f(e).then_inc(h, kk))
        if eng in CHAIN and sem is CHAIN[eng]:
            chain_val[eng] = sem.n
        return sem.n

    ident_b = sbuf("ident_b", [128, 128], BF16)
    ident_f = sbuf("ident_f", [128, 128], F32)
    onesT = sbuf("onesT", [128, 128], BF16)
    ones_f = sbuf("ones_f", [1, 128], F32)
    cst = sbuf("cst", [128, 4], F32)
    bq_s = sbuf("bq_s", [128, 8], F32)
    bk_s = sbuf("bk_s", [128, 8], F32)
    bo_s = sbuf("bo_s", [128, 8], F32)
    bv_s = sbuf("bv_s", [128, 8], F32)
    cb_s = sbuf("cb_s", [128, OCC], F32)
    bb_s = sbuf("bb_s", [128, 32], F32)
    st2 = sbuf("st2", [128, 4], F32)
    rst = sbuf("rst", [128, 1], F32)
    LBP = sbuf("LBP", [128, 4096], BF16)
    ctile = sbuf("ctile", [128, 4], F32)
    exp16 = sbuf("exp16", [16, 128], F32)
    sel16 = sbuf("sel16", [4, 128], F32)
    ohb = sbuf("ohb", [16, 4], BF16)
    ohf = sbuf("ohf", [16, 4], F32)
    iot = sbuf("iot", [16, 1], F32)
    actf = sbuf("actf", [1, 4], F32)
    acts = sbuf("acts", [1, 4], I32)
    pr_s = sbuf("pr_s", [16, 1], F32)
    rb_s = sbuf("rb_s", [16, 1], F32)
    c16 = sbuf("c16", [16, 1], F32)
    csel = sbuf("csel", [1, 4], F32)
    trash2 = sbuf("trash2", [128, D], BF16)

    u1fm = sbuf("u1fm", [128, HC * 1028], BF16)
    cwq = [sbuf("cwq0", [128, 3072], BF16), sbuf("cwq1", [128, 3072], BF16)]
    xf = [sbuf("xf0", [128, D], F32), sbuf("xf1", [128, D], F32)]
    u1t = [sbuf("u1t0", [128, D], BF16), sbuf("u1t1", [128, D], BF16)]
    trash = sbuf("trash", [128, D], BF16)
    ht = sbuf("ht", [4, D], F32)
    hst = sbuf("hst", [4, 4], F32)
    hrst = sbuf("hrst", [4, 1], F32)
    u1h = sbuf("u1h", [4, D], BF16)

    psT = [psum("psT0", [128, 1024], BF16), psum("psT1", [128, 1024], BF16)]
    psS = psum("psS", [128, 512], F32)

    for dst, src in ((ht, xh_d), (acts, ac_d), (pr_s, pr_d), (rb_s, rb_d),
                     (bq_s, bq_d), (bk_s, bk_d), (bo_s, bo_d), (bv_s, bv_d),
                     (cb_s, cb_d), (bb_s, bb_d)):
        run("sync", lambda q, a=dst, b=src: q.dma_start(out=a[:, :], in_=b),
            dS, 16)
    m_in_setup = dS.n

    run("gpsimd", lambda g: g.memset(ident_b[:, :], 1.0), gP)
    run("gpsimd", lambda g: g.affine_select(
        ident_b[:, :], ident_b[:, :], [[-1, 128]], ALU.is_equal,
        0.0, base=0, channel_multiplier=1), gP)
    run("gpsimd", lambda g: g.memset(ident_f[:, :], 1.0), gP)
    run("gpsimd", lambda g: g.affine_select(
        ident_f[:, :], ident_f[:, :], [[-1, 128]], ALU.is_equal,
        0.0, base=0, channel_multiplier=1), gP)
    def g_c1(g):
        g.memset(onesT[:, :], 1.0)
        g.memset(ones_f[:, :], 1.0)
        g.memset(cst[:, 0:1], EPS_LN)
        g.memset(cst[:, 1:2], -0.5)
        g.memset(cst[:, 2:3], EPS_PRIOR)
        g.memset(cst[:, 3:4], 1.0 / D)
        g.iota(iot[:, :], [[0, 1]], base=0, channel_multiplier=1,
               allow_small_or_imprecise_dtypes=True)
        return g.memset(exp16[:, :], 1.0)
    run("gpsimd", g_c1, gP)
    run("gpsimd", lambda g: g.affine_select(
        exp16[:, :], exp16[:, :], [[1, 128]], ALU.is_ge, 0.0,
        base=0, channel_multiplier=-8), gP)
    m_g_const = run("gpsimd", lambda g: g.affine_select(
        exp16[:, :], exp16[:, :], [[-1, 128]], ALU.is_ge, 0.0,
        base=7, channel_multiplier=8), gP)

    run("gpsimd", lambda g: g.dma_start(LBP[32:33, 0:4096], pb_d), dWb, 16)
    m_pb = dWb.n

    def ln_stats(xin, stats, rstd, nparts, vwait=None):
        ms = run("scalar", lambda s: s.activation(
            trash[0:nparts, :], xin, AF.Square, accum_out=stats[:, 1:2]), sP)
        if vwait is not None:
            wait("vector", vwait[0], vwait[1])
        run("vector", lambda v, nparts=nparts: v.tensor_scalar(
            trash2[0:nparts, :], xin, 1.0, 0.0, ALU.mult, ALU.add,
            accum_out=stats[:, 0:1]), vP)
        wait("vector", sP, ms)
        m1 = run("vector", lambda v: v.tensor_scalar_mul(
            stats[:, 2:3], stats[:, 0:1], 1.0 / D), vP)
        wait("vector", vP, m1)
        m2 = run("vector", lambda v: v.tensor_scalar_mul(
            stats[:, 0:1], stats[:, 1:2], 1.0 / D), vP)
        wait("vector", vP, m2)
        m3 = run("vector", lambda v: v.tensor_tensor(
            stats[:, 1:2], stats[:, 2:3], stats[:, 2:3], ALU.mult), vP)
        wait("vector", vP, m3)
        m4 = run("vector", lambda v: v.tensor_sub(
            stats[:, 3:4], stats[:, 0:1], stats[:, 1:2]), vP)
        wait("scalar", vP, m4)
        run("scalar", lambda s: s.activation(
            rstd[:, :], stats[0:nparts, 3:4], AF.Ln,
            bias=cst[0:nparts, 0:1]), sP)
        return run("scalar", lambda s: s.activation(
            rstd[:, :], rstd[:, :], AF.Exp, scale=cst[0:nparts, 1:2]), sP)

    norm_m = {}
    norm_s = {}
    tev_m = {}
    xin_m = {}

    ms0_m = {}
    vs0_m = {}
    vs5_m = {}

    def emit_ln0a(t):
        wait("scalar", dX[t % 2], xin_m[t])
        ms0_m[t] = run("scalar", lambda s, t=t: s.activation(
            trash[:, :], xf[t % 2][:, :], AF.Square,
            accum_out=st2[:, 1:2]), sP)
        wait("vector", dX[t % 2], xin_m[t])
        vs0_m[t] = run("vector", lambda v, t=t: v.tensor_scalar(
            trash2[:, :], xf[t % 2][:, :], 1.0, 0.0, ALU.mult, ALU.add,
            accum_out=st2[:, 0:1]), vP)

    def emit_ln0b(t):
        wait("vector", sP, ms0_m[t])
        wait("scalar", vP, vs0_m[t])
        run("scalar", lambda s: s.activation(
            st2[:, 2:3], st2[:, 0:1], AF.Identity, scale=cst[:, 3:4]), sP)
        run("scalar", lambda s: s.activation(
            st2[:, 0:1], st2[:, 2:3], AF.Square), sP)
        mq = run("scalar", lambda s: s.activation(
            st2[:, 1:2], st2[:, 1:2], AF.Identity, scale=cst[:, 3:4],
            bias=cst[:, 0:1]), sP)
        wait("vector", sP, mq)
        m4 = run("vector", lambda v: v.tensor_sub(
            st2[:, 3:4], st2[:, 1:2], st2[:, 0:1]), vP)
        wait("scalar", vP, m4)
        run("scalar", lambda s: s.activation(
            rst[:, :], st2[:, 3:4], AF.Ln), sP)
        mr = run("scalar", lambda s: s.activation(
            rst[:, :], rst[:, :], AF.Exp, scale=cst[:, 1:2]), sP)
        wait("vector", sP, mr)
        mb = run("vector", lambda v: v.tensor_scalar(
            st2[:, 0:1], st2[:, 2:3], rst[:, 0:1], -1.0,
            ALU.mult, ALU.mult), vP)
        norm_m[t] = run("vector", lambda v, t=t: v.tensor_scalar(
            u1t[t % 2][:, 0:2048], xf[t % 2][:, 0:2048],
            st2[:, 2:3], rst[:, 0:1], ALU.subtract, ALU.mult), vP)
        wait("scalar", vP, mb)
        norm_s[t] = run("scalar", lambda s, t=t: s.activation(
            u1t[t % 2][:, 2048:4096], xf[t % 2][:, 2048:4096], AF.Identity,
            scale=rst[:, 0:1], bias=st2[:, 0:1]), sP)

    def emit_ln0(t):
        emit_ln0a(t)
        emit_ln0b(t)

    def emit_xdma(t):
        if t >= 2:
            wait("sync", vP, norm_m[t - 2])
            wait("sync", sP, norm_s[t - 2])
        run("sync", lambda q, t=t: q.dma_start(
            out=xf[t % 2][:, :], in_=x_d[t * 128:(t + 1) * 128, :]),
            dX[t % 2], 16)
        xin_m[t] = dX[t % 2].n

    wait("scalar", gP, m_g_const)
    wait("vector", gP, m_g_const)
    gidx = 0
    for t in range(NT):
        if t == 0:
            emit_xdma(0)
            emit_xdma(1)
            emit_ln0(0)
        if t + 2 < NT:
            emit_xdma(t + 2)

        scol = (t // 4) * 514 + 1 + (t % 4) * 128
        for grp in range(4):
            if grp < 2:
                wait("tensor", vP, norm_m[t])
            else:
                wait("tensor", sP, norm_s[t])
            if gidx >= 2:
                wait("tensor", vP, tev_m[gidx - 2])
            def t_grp(tt, t=t, grp=grp, par=gidx % 2):
                last = None
                for j in range(8):
                    hcc = grp * 8 + j
                    last = tt.transpose(psT[par][:, j * 128:(j + 1) * 128],
                                        u1t[t % 2][:, hcc * 128:(hcc + 1) * 128],
                                        ident_b[:, :])
                return last
            mT = run("tensor", t_grp, tP)
            wait("vector", tP, mT)
            def v_grp(v, grp=grp, scol=scol, par=gidx % 2):
                o = bass.AP(u1fm, (grp * 8) * 1028 + scol,
                            [[HC * 1028, 128], [1028, 8], [1, 128]])
                i = bass.AP(psT[par], 0, [[1024, 128], [128, 8], [1, 128]])
                return v.tensor_copy(o, i)
            tev_m[gidx] = run("vector", v_grp, vP)
            gidx += 1
            if grp == 1 and t + 1 < NT:
                emit_ln0a(t + 1)
            if grp == 3 and t + 1 < NT:
                emit_ln0b(t + 1)

    wait("scalar", dS, m_in_setup)
    mr = ln_stats(ht[:, :], hst, hrst, 4, vwait=(dS, m_in_setup))
    wait("vector", sP, mr)
    m_u1h = run("vector", lambda v: v.tensor_scalar(
        u1h[:, :], ht[:, :], hst[:, 2:3], hrst[:, 0:1],
        ALU.subtract, ALU.mult), vP)
    wait("tensor", vP, m_u1h)
    wait("tensor", gP, m_g_const)
    wait("tensor", vP, tev_m[gidx - 1])
    wait("tensor", vP, tev_m[gidx - 2])
    def t_halo(t):
        last = None
        for hcc in range(HC):
            last = t.transpose(psT[0][:, hcc * 4: hcc * 4 + 4],
                               u1h[:, hcc * 128:(hcc + 1) * 128],
                               ident_b[0:4, 0:4])
        return last
    m_haloT = run("tensor", t_halo, tP)
    wait("vector", tP, m_haloT)
    def v_halo_evac(v):
        o0 = bass.AP(u1fm, 0, [[HC * 1028, 128], [1028, HC], [513, 2]])
        i0 = bass.AP(psT[0], 0, [[1024, 128], [4, HC], [1, 2]])
        v.tensor_copy(o0, i0)
        o1 = bass.AP(u1fm, 514, [[HC * 1028, 128], [1028, HC], [513, 2]])
        i1 = bass.AP(psT[0], 2, [[1024, 128], [4, HC], [1, 2]])
        return v.tensor_copy(o1, i1)
    m_halo_ev = run("vector", v_halo_evac, vP)

    wait("vector", dS, m_in_setup)
    m_actf = run("vector", lambda v: v.tensor_copy(actf[:, :], acts[:, :]), vP)
    wait("tensor", vP, m_actf)
    wait("tensor", gP, m_g_const)
    m_ohp = run("tensor", lambda t: t.matmul(
        psS[0:16, 0:4], ones_f[0:1, 0:16], actf[:, :], start=True, stop=True),
        tP)
    wait("vector", tP, m_ohp)
    run("vector", lambda v: v.tensor_tensor(
        ohf[:, :], psS[0:16, 0:4], iot[:, :].broadcast_to([16, 4]),
        ALU.is_equal), vP)
    m_oh = run("vector", lambda v: v.tensor_copy(ohb[:, :], ohf[:, :]), vP)

    wait("scalar", dS, m_in_setup)
    wait("scalar", gP, m_g_const)
    m_lnp = run("scalar", lambda s: s.activation(
        c16[:, :], pr_s[:, :], AF.Ln, bias=cst[0:16, 2:3]), sP)
    wait("vector", sP, m_lnp)
    m_c16 = run("vector", lambda v: v.tensor_add(
        c16[:, :], c16[:, :], rb_s[:, :]), vP)
    wait("tensor", vP, m_c16)
    m_cs = run("tensor", lambda t: t.matmul(
        psS[0:1, 4:8], c16[:, :], ohf[:, :], start=True, stop=True), tP)
    wait("vector", tP, m_cs)
    m_cse = run("vector", lambda v: v.tensor_copy(csel[:, :], psS[0:1, 4:8]),
                vP)
    wait("tensor", vP, m_cse)
    m_ct = run("tensor", lambda t: t.matmul(
        psS[:, 8:12], ones_f[:, :], csel[:, :], start=True, stop=True), tP)
    wait("vector", tP, m_ct)
    m_ctile = run("vector", lambda v: v.tensor_copy(ctile[:, :], psS[:, 8:12]),
                  vP)

    wait("tensor", vP, m_oh)
    wait("tensor", vP, m_ctile)
    m_s16 = run("tensor", lambda t: t.matmul(
        psS[0:4, 0:128], ohf[:, :], exp16[:, :], start=True, stop=True), tP)
    wait("vector", tP, m_s16)
    m_sel16 = run("vector", lambda v: v.tensor_copy(
        sel16[:, :], psS[0:4, 0:128]), vP)

    m_p0_done_v = vP.n
    m_p0_done_t = tP.n

    def _finish_early(sem, val):
        wait("sync", sem, val)
        run("sync", lambda q: q.dma_start(out_d[0:128, 0:128],
                                          ident_f[:, 0:128]), dO[0], 16)
        wait("sync", dO[0], dO[0].n)
        block = es.enter_context(nc.Block())
        for eng in ENGS:
            def _runner(e, fns=tuple(prog[eng])):
                for f in fns:
                    f(e)
            getattr(block, eng)(_runner)
        for name in reversed(list(scopes)):
            scopes.pop(name).close()
        es.close()
        return nc

    if _STOP == "P0":
        return _finish_early(vP, m_p0_done_v)

    free("u1h", "hrst", "hst", "ht",
         "trash", "u1t1", "u1t0", "xf1", "xf0")

    qkvfm = sbuf("qkvfm", [128, OCC * 1024], BF16, side="right")
    wq_s = sbuf("wq_s", [128, 8192], BF16, side="right")
    wk_s = sbuf("wk_s", [128, 8192], BF16, side="right")
    psC = [psum("psC0", [128, 1024], F32), psum("psC1", [128, 1024], F32)]

    qdma_m = {}
    qte_m = {}
    cev_m = {}
    for oc in range(OCC):
        for q in range(4):
            k = oc * 4 + q
            if k in (48, 52, 56, 60):
                if k == 48:
                    wait("gpsimd", tP, m_p0_done_t)
                i4 = (k - 48) // 4
                wsrc = (wq_d, wq_d, wk_d, wk_d)[i4]
                wdst = (wq_s, wq_s, wk_s, wk_s)[i4]
                hlf = (i4 % 2) * 4096
                run("gpsimd", lambda g, wsrc=wsrc, wdst=wdst, hlf=hlf:
                    g.dma_start(wdst[:, hlf:hlf + 4096],
                                wsrc[:, hlf:hlf + 4096]), dW2, 16)
            if k >= 2:
                wait("gpsimd", tP, qte_m[k - 2])
            run("gpsimd", lambda g, oc=oc, q=q, k=k: g.dma_start(
                cwq[k % 2][:, :], cw_d[oc, :, q * 3072:(q + 1) * 3072]),
                dWq[k % 2], 16)
            qdma_m[k] = dWq[k % 2].n
            wait("tensor", dWq[k % 2], qdma_m[k])
            if q == 0:
                wait("tensor", vP, cev_m[oc - 2] if oc >= 2 else m_p0_done_v)
            def t_conv(t, q=q, k=k, oc=oc):
                last = None
                for gi in range(24):
                    g = q * 24 + gi
                    w, hcc = g // HC, g % HC
                    for seg in range(2):
                        base = hcc * 1028 + seg * 514 + w
                        last = t.matmul(
                            psC[oc % 2][:, seg * 512:(seg + 1) * 512],
                            cwq[k % 2][:, gi * 128:(gi + 1) * 128],
                            u1fm[:, base: base + 512],
                            start=(g == 0), stop=(g == 95))
                return last
            qte_m[k] = run("tensor", t_conv, tP)
        wait("vector", tP, qte_m[oc * 4 + 3])
        cev_m[oc] = run("vector", lambda v, oc=oc: v.tensor_scalar_add(
            qkvfm[:, oc * 1024:(oc + 1) * 1024], psC[oc % 2][:, :],
            cb_s[:, oc:oc + 1]), vP)

    m_conv_t = tP.n
    m_conv_v = vP.n
    if _STOP == "P1":
        return _finish_early(vP, m_conv_v)

    free("cwq1", "cwq0", "u1fm")
    wo_s = sbuf("wo_s", [128, 8192], BF16)
    qh_s = sbuf("qh_s", [128, 8192], BF16)
    kh_s = sbuf("kh_s", [128, 8192], BF16)
    vh_s = sbuf("vh_s", [128, 8192], BF16)
    o_fm = sbuf("o_fm", [128, 8192], BF16)
    wv_s = sbuf("wv_s", [128, 8192], BF16)

    wait("gpsimd", tP, m_conv_t)
    run("gpsimd", lambda g: g.dma_start(wv_s[:, :], wv_d), dW2b, 16)
    run("gpsimd", lambda g: g.dma_start(wo_s[:, :], wo_d), dW2b, 16)
    m_wq = m_wk = 64
    m_wv = m_wo = 32

    p2ev = {}
    p2i = 0
    for kind in range(3):
        wm = (m_wq, m_wk, m_wv)[kind]
        ws = (dW2, dW2, dW2b)[kind]
        for m in range(8):
            wait("tensor", ws, wm)
            wait("tensor", vP, p2ev[p2i - 2] if p2i >= 2 else m_conv_v)
            def t_p2(t, kind=kind, m=m, par=p2i % 2):
                last = None
                wsrc = (wq_s, wk_s, wv_s)[kind]
                for k in range(8):
                    for hf in range(2):
                        if kind < 2:
                            lhs = wsrc[:, (k * 8 + m) * 128:
                                       (k * 8 + m + 1) * 128]
                            rhs = qkvfm[:, (kind * 8 + k) * 1024 + hf * 512:
                                        (kind * 8 + k) * 1024 + (hf + 1) * 512]
                        else:
                            lhs = qkvfm[:, (16 + k) * 1024 + m * 128:
                                        (16 + k) * 1024 + (m + 1) * 128]
                            rhs = wsrc[:, k * 1024 + hf * 512:
                                       k * 1024 + (hf + 1) * 512]
                        last = t.matmul(
                            psC[par][:, hf * 512:(hf + 1) * 512],
                            lhs, rhs, start=(k == 0), stop=(k == 7))
                return last
            mte = run("tensor", t_p2, tP)
            wait("vector", tP, mte)
            if kind == 0:
                fn = (lambda v, m=m, par=p2i % 2: v.tensor_scalar(
                    qh_s[:, m * 1024:(m + 1) * 1024], psC[par][:, :],
                    bq_s[:, m:m + 1], 0.125, ALU.add, ALU.mult))
            elif kind == 1:
                fn = (lambda v, m=m, par=p2i % 2: v.tensor_scalar_add(
                    kh_s[:, m * 1024:(m + 1) * 1024], psC[par][:, :],
                    bk_s[:, m:m + 1]))
            else:
                fn = (lambda v, m=m, par=p2i % 2: v.tensor_copy(
                    vh_s[:, m * 1024:(m + 1) * 1024], psC[par][:, :]))
            p2ev[p2i] = run("vector", fn, vP)
            p2i += 1
    m_p2_t = tP.n
    m_p2_v = vP.n
    if _STOP == "P2":
        return _finish_early(vP, m_p2_v)

    free("wv_s")
    free("wk_s", "wq_s", "qkvfm")
    free("psC1", "psC0", "psS", "psT1", "psT0")
    mask16 = sbuf("mask16", [128, 2048], F32)
    sm_s = sbuf("sm_s", [128, 2048], F32)
    ss_s = sbuf("ss_s", [128, 2048], F32)
    p32_s = sbuf("p32_s", [128, 2048], F32)
    pbf_s = sbuf("pbf_s", [128, 2048], BF16)
    pT_s = sbuf("pT_s", [128, 2048], BF16)
    nm16 = sbuf("nm16", [128, 16], F32)
    rs16 = sbuf("rs16", [128, 16], F32)
    ri16 = sbuf("ri16", [128, 16], F32)
    pw_s = sbuf("pw_s", [128, 32768], BF16, side="right")

    psA = psum("psA", [128, 2048], F32)
    psPT = psum("psPT", [128, 2048], BF16)
    psO = psum("psO", [128, 1024], F32)

    wait("gpsimd", tP, m_p2_t)
    run("gpsimd", lambda g: g.memset(mask16[:, :], 0.0), gP)
    m_mask = run("gpsimd", lambda g: g.affine_select(
        mask16[:, :], mask16[:, :], [[0, 16], [-1, 128]],
        ALU.is_ge, NEG, base=0, channel_multiplier=1), gP)
    for i in range(4):
        run("gpsimd", lambda g, i=i: g.dma_start(
            pw_s[:, i * 8192:(i + 1) * 8192],
            pw_d[:, i * 8192:(i + 1) * 8192]), dW3, 16)
    m_pw = dW3.n

    smv = {}; pbfv = {}; ptev = {}; oev = {}; expv = {}
    ptm = {}; pvm = {}; scm = {}

    def emit_sc(b):
        if b == 0:
            wait("tensor", vP, m_p2_v)
        else:
            wait("tensor", vP, smv[(b - 1, 1)])
        def t_sc(t, b=b):
            last = None
            for h in list(range(0, 16, 2)) + list(range(1, 16, 2)):
                r0 = 64 * (h % 2)
                cc = (h // 2) * 1024 + b * 128
                last = t.matmul(psA[:, h * 128:(h + 1) * 128],
                                qh_s[r0:r0 + 64, cc:cc + 128],
                                kh_s[r0:r0 + 64, cc:cc + 128],
                                start=True, stop=True)
            return last
        scm[b] = run("tensor", t_sc, tP)

    for b in range(NT):
        if b == 0:
            emit_sc(0)
        m_sc = scm[b]

        for half in range(2):
            c0 = half * 1024
            h0 = half * 8
            wait("vector", tP, m_sc)
            if b == 0 and half == 0:
                wait("vector", gP, m_mask)
            if b > 0:
                wait("vector", sP, expv[(b - 1, half)])
            smv[(b, half)] = run("vector", lambda v, c0=c0: v.tensor_add(
                sm_s[:, c0:c0 + 1024], psA[:, c0:c0 + 1024],
                mask16[:, c0:c0 + 1024]), vP)
            run("vector", lambda v, c0=c0, h0=h0: v.reduce_max(
                nm16[:, h0:h0 + 8],
                sm_s[:, c0:c0 + 1024].rearrange("p (a b) -> p a b", b=128),
                axis=AX.X, negate=True), vP)
            nmm = vP.n

            wait("scalar", vP, nmm)
            if b > 0:
                wait("scalar", vP, pbfv[(b - 1, half)])
            def s_exp(s, h0=h0):
                last = None
                for h in range(h0, h0 + 8):
                    last = s.activation(
                        p32_s[:, h * 128:(h + 1) * 128],
                        sm_s[:, h * 128:(h + 1) * 128], AF.Exp,
                        bias=nm16[:, h:h + 1], accum_out=rs16[:, h:h + 1])
                return last
            expv[(b, half)] = run("scalar", s_exp, sP)

        for half in range(2):
            c0 = half * 1024
            h0 = half * 8
            wait("vector", sP, expv[(b, half)])
            run("vector", lambda v, h0=h0: v.reciprocal(
                ri16[:, h0:h0 + 8], rs16[:, h0:h0 + 8]), vP)
            m_rc = vP.n
            wait("vector", vP, m_rc)
            if b > 0:
                wait("vector", tP, ptm[(b - 1, half)])
            pbfv[(b, half)] = run("vector", lambda v, c0=c0, h0=h0: v.tensor_tensor(
                pbf_s[:, c0:c0 + 1024].rearrange("p (a b) -> p a b", b=128),
                p32_s[:, c0:c0 + 1024].rearrange("p (a b) -> p a b", b=128),
                ri16[:, h0:h0 + 8].unsqueeze(2).broadcast_to([128, 8, 128]),
                ALU.mult), vP)

        for half in range(2):
            c0 = half * 1024
            h0 = half * 8
            if half == 1 and b + 1 < NT:
                emit_sc(b + 1)
            wait("tensor", vP, pbfv[(b, half)])
            if b > 0:
                wait("tensor", sP, ptev[(b - 1, half)])
            def t_pt(t, h0=h0):
                last = None
                for h in range(h0, h0 + 8):
                    last = t.transpose(psPT[:, h * 128:(h + 1) * 128],
                                       pbf_s[:, h * 128:(h + 1) * 128],
                                       ident_b[:, :])
                return last
            ptm[(b, half)] = run("tensor", t_pt, tP)

            wait("scalar", tP, ptm[(b, half)])
            if b > 0:
                wait("scalar", tP, pvm[(b - 1, half)])
            ptev[(b, half)] = run("scalar", lambda s, c0=c0: s.copy(
                pT_s[:, c0:c0 + 1024], psPT[:, c0:c0 + 1024]), sP)

            wait("tensor", sP, ptev[(b, half)])
            if b > 0:
                wait("tensor", vP, oev[(b - 1, half)])
            def t_pv(t, b=b, h0=h0):
                last = None
                for h in range(h0, h0 + 8):
                    r0 = 64 * (h % 2)
                    cc = (h // 2) * 128
                    last = t.matmul(psO[r0:r0 + 64, cc:cc + 128],
                                    vh_s[:, b * 1024 + h * 64:
                                         b * 1024 + (h + 1) * 64],
                                    pT_s[:, h * 128:(h + 1) * 128],
                                    start=True, stop=True,
                                    tile_position=(0, r0))
                return last
            pvm[(b, half)] = run("tensor", t_pv, tP)

            wait("vector", tP, pvm[(b, half)])
            def v_oe(v, b=b, half=half):
                last = None
                for m in range(half * 4, half * 4 + 4):
                    last = v.tensor_scalar_add(
                        o_fm[:, m * 1024 + b * 128: m * 1024 + (b + 1) * 128],
                        psO[:, m * 128:(m + 1) * 128], bv_s[:, m:m + 1])
                return last
            oev[(b, half)] = run("vector", v_oe, vP)
    m_p3_t = tP.n
    m_p3_v = vP.n
    if _STOP == "P3":
        return _finish_early(vP, vP.n)

    free("ri16", "rs16", "nm16", "pT_s", "pbf_s", "p32_s", "ss_s", "sm_s",
         "mask16")
    free("psO", "psPT", "psA")
    o2_s = sbuf("o2_s", [128, 8192], BF16, side="right")
    psC2 = [psum("psC2a", [128, 1024], F32), psum("psC2b", [128, 1024], F32)]

    p4ev = {}
    for m in range(8):
        wait("tensor", dW2b, m_wo)
        wait("tensor", vP, p4ev[m - 2] if m >= 2 else m_p3_v)
        def t_p4(t, m=m):
            last = None
            for k in range(8):
                for hf in range(2):
                    last = t.matmul(
                        psC2[m % 2][:, hf * 512:(hf + 1) * 512],
                        wo_s[:, (k * 8 + m) * 128:(k * 8 + m + 1) * 128],
                        o_fm[:, k * 1024 + hf * 512: k * 1024 + (hf + 1) * 512],
                        start=(k == 0), stop=(k == 7))
            return last
        mte = run("tensor", t_p4, tP)
        wait("vector", tP, mte)
        p4ev[m] = run("vector", lambda v, m=m: v.tensor_scalar_add(
            o2_s[:, m * 1024:(m + 1) * 1024], psC2[m % 2][:, :],
            bo_s[:, m:m + 1]), vP)
    m_p4_t = tP.n
    m_p4_v = vP.n
    if _STOP == "P4":
        return _finish_early(vP, m_p4_v)

    free("wv_s") if False else None
    free("o_fm", "vh_s", "kh_s", "qh_s", "wo_s")
    free("psC2b", "psC2a")
    u2tm = sbuf("u2tm", [128, 32768], BF16)
    x2 = [sbuf("x2a", [128, D], F32), sbuf("x2b", [128, D], F32)]
    xq = [sbuf("xqa", [128, 1024], F32), sbuf("xqb", [128, 1024], F32)]
    psP = psum("psP", [128, 1024], F32)

    xq_m = {}
    tt_m = {}
    st5_m = {}
    nrm5_m = {}
    x2dma_m = {}
    def emit_xq(t, qb):
        if t == 0 and qb == 0:
            wait("gpsimd", tP, m_p4_t)
        if qb >= 2:
            wait("gpsimd", vP, tt_m[(t, (qb - 2) * 2 + 1)])
        elif t >= 1:
            wait("gpsimd", vP, tt_m[(t - 1, (qb + 2) * 2 + 1)])
        run("gpsimd", lambda g, t=t, qb=qb: g.dma_start(
            xq[qb % 2][:, :],
            x_d[t * 128:(t + 1) * 128, qb * 1024:(qb + 1) * 1024]),
            dX5[qb % 2], 16)
        xq_m[(t, qb)] = dX5[qb % 2].n

    st5a_m = {}

    def emit_ln5a(t):
        wait("scalar", vP, tt_m[(t, 7)])
        st5_m[t] = run("scalar", lambda s, t=t: s.activation(
            u2tm[:, t * 4096:(t + 1) * 4096], x2[t % 2][:, :],
            AF.Square, accum_out=st2[:, 1:2]), sP)
        vs5_m[t] = run("vector", lambda v, t=t: v.tensor_scalar(
            trash2[:, :], x2[t % 2][:, :], 1.0, 0.0,
            ALU.mult, ALU.add, accum_out=st2[:, 0:1]), vP)

    def emit_ln5b(t):
        wait("vector", sP, st5_m[t])
        wait("scalar", vP, vs5_m[t])
        run("scalar", lambda s: s.activation(
            st2[:, 2:3], st2[:, 0:1], AF.Identity, scale=cst[:, 3:4]), sP)
        run("scalar", lambda s: s.activation(
            st2[:, 0:1], st2[:, 2:3], AF.Square), sP)
        mq = run("scalar", lambda s: s.activation(
            st2[:, 1:2], st2[:, 1:2], AF.Identity, scale=cst[:, 3:4],
            bias=cst[:, 0:1]), sP)
        wait("vector", sP, mq)
        m4 = run("vector", lambda v: v.tensor_sub(
            st2[:, 3:4], st2[:, 1:2], st2[:, 0:1]), vP)
        wait("scalar", vP, m4)
        run("scalar", lambda s: s.activation(
            rst[:, :], st2[:, 3:4], AF.Ln), sP)
        mr5 = run("scalar", lambda s: s.activation(
            rst[:, :], rst[:, :], AF.Exp, scale=cst[:, 1:2]), sP)
        wait("vector", sP, mr5)
        nrm5_m[t] = run("vector", lambda v, t=t: v.tensor_scalar(
            u2tm[:, t * 4096:(t + 1) * 4096], x2[t % 2][:, :],
            st2[:, 2:3], rst[:, 0:1], ALU.subtract, ALU.mult), vP)

    for t in range(NT):
        emit_xq(t, 0)
        emit_xq(t, 1)
        for bk in range(8):
            if bk == 4:
                emit_xq(t, 2)
                if t >= 1:
                    emit_ln5a(t - 1)
            if bk == 6 and t >= 1:
                emit_ln5b(t - 1)
            if bk == 6:
                emit_xq(t, 3)
            gi = t * 8 + bk
            wait("tensor", dW3, m_pw)
            wait("tensor", dWb, m_pb)
            if gi >= 2:
                wait("tensor", vP, tt_m[(t, bk - 2)] if bk >= 2
                     else tt_m[(t - 1, 6 + bk)])
            else:
                wait("tensor", vP, m_p4_v)
            if bk == 0 and t >= 2:
                wait("tensor", sP, st5_m[t - 2])
                wait("tensor", vP, nrm5_m[t - 2])
                wait("tensor", dO[t % 2], x2dma_m[t - 2])
            def t_p5(t_, t=t, bk=bk):
                par = bk % 2
                for r in range(8):
                    t_.matmul(psP[:, par * 512:(par + 1) * 512],
                              o2_s[:, r * 1024 + t * 128:
                                   r * 1024 + (t + 1) * 128],
                              pw_s[:, r * 4096 + bk * 512:
                                   r * 4096 + (bk + 1) * 512],
                              start=(r == 0), stop=False)
                return t_.matmul(psP[:, par * 512:(par + 1) * 512],
                                 onesT[32:33, 0:128],
                                 LBP[32:33, bk * 512:(bk + 1) * 512],
                                 start=False, stop=True)
            mte = run("tensor", t_p5, tP)
            wait("vector", tP, mte)
            wait("vector", dX5[(bk // 2) % 2], xq_m[(t, bk // 2)])
            if t >= 2 and bk == 0:
                wait("vector", sP, st5_m[t - 2])
                wait("vector", dO[t % 2], x2dma_m[t - 2])
            tt_m[(t, bk)] = run("vector", lambda v, t=t, bk=bk: v.tensor_tensor(
                x2[t % 2][:, bk * 512:(bk + 1) * 512],
                psP[:, (bk % 2) * 512:((bk % 2) + 1) * 512],
                xq[(bk // 2) % 2][:, (bk % 2) * 512:((bk % 2) + 1) * 512],
                ALU.add), vP)
            if bk % 2 == 1:
                qq = bk // 2
                wait("sync", vP, tt_m[(t, bk)])
                run("sync", lambda q, t=t, qq=qq: q.dma_start(
                    out_d[t * 128:(t + 1) * 128, qq * 1024:(qq + 1) * 1024],
                    x2[t % 2][:, qq * 1024:(qq + 1) * 1024]), dO[t % 2], 16)

        x2dma_m[t] = dO[t % 2].n
    emit_ln5a(NT - 1)
    emit_ln5b(NT - 1)
    dO_all = (dO[0].n, dO[1].n)
    m_p5_t = tP.n
    if _STOP == "P5":
        return _finish_early(vP, nrm5_m[NT - 1])

    free("xqb", "xqa", "x2b", "x2a")
    free("o2_s", "pw_s")
    free("psP")
    u2fm = sbuf("u2fm", [128, 32768], BF16, side="right")
    la2 = sbuf("la2", [128, 4096], BF16, side="right")
    lball = sbuf("lball", [128, 4096], BF16, side="right")
    rwt = sbuf("rwt", [128, 512], BF16, side="right")
    wait("gpsimd", tP, m_p5_t)
    run("gpsimd", lambda g: g.dma_start(la2[:, :], la_d), dWa, 16)
    run("gpsimd", lambda g: g.dma_start(lball[:, :], lb_d), dWa, 16)
    run("gpsimd", lambda g: g.dma_start(rwt[:, :], rw_d), dWa, 16)
    m_lw2 = dWa.n
    psT2 = [psum("psT2a", [128, 1024], BF16), psum("psT2b", [128, 1024], BF16)]

    g2 = 0
    tev2 = {}
    for t in range(NT):
        for grp in range(4):
            wait("tensor", vP, nrm5_m[t])
            if g2 < 2:
                wait("tensor", vP, tt_m[(NT - 1, 7)])
            if g2 >= 2:
                wait("tensor", vP, tev2[g2 - 2])
            def t_g2(tt, t=t, grp=grp, par=g2 % 2):
                last = None
                for j in range(8):
                    hcc = grp * 8 + j
                    last = tt.transpose(psT2[par][:, j * 128:(j + 1) * 128],
                                        u2tm[:, t * 4096 + hcc * 128:
                                             t * 4096 + (hcc + 1) * 128],
                                        ident_b[:, :])
                return last
            mT = run("tensor", t_g2, tP)
            wait("vector", tP, mT)
            def v_g2(v, t=t, grp=grp, par=g2 % 2):
                o = bass.AP(u2fm, (grp * 8) * 1024 + t * 128,
                            [[32768, 128], [1024, 8], [1, 128]])
                i = bass.AP(psT2[par], 0, [[1024, 128], [128, 8], [1, 128]])
                return v.tensor_copy(o, i)
            tev2[g2] = run("vector", v_g2, vP)
            g2 += 1
    m_p5b_v = vP.n
    if _STOP == "P5b":
        return _finish_early(vP, m_p5b_v)

    free("u2tm")
    free("psT2b", "psT2a")
    lgf = sbuf("lgf", [4, 1024], F32)
    lga = sbuf("lga", [16, 1024], BF16)
    alf = sbuf("alf", [4, 1024], F32)
    alfb = sbuf("alfb", [4, 1024], BF16)
    zscB = sbuf("zscB", [128, 1024], BF16)
    ua4 = sbuf("ua4", [128, 4], F32)
    al4 = sbuf("al4", [128, 4], F32)
    nm4 = sbuf("nm4", [128, 1], F32)
    rc4 = sbuf("rc4", [128, 1], F32)
    ae_s = sbuf("ae_s", [128, 1024], F32)
    sel16b = sbuf("sel16b", [4, 128], BF16)
    psZ = psum("psZ", [128, 1024], F32)
    psL = psum("psL", [16, 1024], F32)
    psM = psum("psM", [128, 512], F32)
    psE = psum("psE", [128, 1024], F32)

    wait("tensor", vP, m_p5b_v)
    wait("tensor", dWa, m_lw2)
    def t_z(t):
        last = None
        for hcc in range(HC):
            for hf in range(2):
                last = t.matmul(psZ[:, hf * 512:(hf + 1) * 512],
                                la2[:, hcc * 128:(hcc + 1) * 128],
                                u2fm[:, hcc * 1024 + hf * 512:
                                     hcc * 1024 + (hf + 1) * 512],
                                start=(hcc == 0), stop=(hcc == HC - 1))
        return last
    m_z = run("tensor", t_z, tP)

    def t_lg(t):
        last = None
        for hcc in range(HC):
            for hf in range(2):
                last = t.matmul(psL[:, hf * 512:(hf + 1) * 512],
                                rwt[:, hcc * 16:(hcc + 1) * 16],
                                u2fm[:, hcc * 1024 + hf * 512:
                                     hcc * 1024 + (hf + 1) * 512],
                                start=(hcc == 0), stop=(hcc == HC - 1))
        return last
    m_lg = run("tensor", t_lg, tP)
    wait("vector", tP, m_lg)
    wait("vector", dO[0], dO_all[0])
    wait("vector", dO[1], dO_all[1])
    m_lga = run("vector", lambda v: v.tensor_copy(lga[:, :], psL[:, :]), vP)
    wait("vector", vP, m_lga)
    m_ohb2 = run("vector", lambda v: v.tensor_copy(sel16b[:, :], sel16[:, :]),
                 vP)
    wait("tensor", vP, m_ohb2)
    def t_sel(t):
        t.matmul(psM[0:4, 0:512], ohb[:, :], lga[:, 0:512],
                 start=True, stop=True)
        return t.matmul(psM[0:4, 0:512][0:4, 0:512] if False else
                        psL[0:4, 0:512], ohb[:, :], lga[:, 512:1024],
                        start=True, stop=True)
    m_self = run("tensor", t_sel, tP)
    wait("vector", tP, m_self)
    def v_self(v):
        v.tensor_copy(lgf[:, 0:512], psM[0:4, 0:512])
        return v.tensor_copy(lgf[:, 512:1024], psL[0:4, 0:512])
    m_lgf = run("vector", v_self, vP)

    alf_m = {}
    for t in range(NT):
        wait("tensor", vP, m_lgf if t == 0 else alf_m[t - 1])
        m_t1 = run("tensor", lambda tt, t=t: tt.transpose(
            psM[:, 0:4], lgf[:, t * 128:(t + 1) * 128], ident_f[0:4, 0:4]), tP)
        wait("vector", tP, m_t1)
        m_ua = run("vector", lambda v: v.tensor_add(
            ua4[:, :], psM[:, 0:4], ctile[:, :]), vP)
        wait("vector", vP, m_ua)
        m_n4 = run("vector", lambda v: v.reduce_max(
            nm4[:, :], ua4[:, :], axis=AX.X, negate=True), vP)
        wait("scalar", vP, m_n4)
        m_e4 = run("scalar", lambda s: s.activation(
            al4[:, :], ua4[:, :], AF.Exp, bias=nm4[:, 0:1]), sP)
        wait("vector", sP, m_e4)
        m_s4 = run("vector", lambda v: v.reduce_sum(
            nm4[:, :], al4[:, :], axis=AX.X), vP)
        wait("vector", vP, m_s4)
        m_r4 = run("vector", lambda v: v.reciprocal(rc4[:, :], nm4[:, :]), vP)
        wait("vector", vP, m_r4)
        m_a4 = run("vector", lambda v: v.tensor_scalar_mul(
            ua4[:, :], al4[:, :], rc4[:, 0:1]), vP)
        wait("tensor", vP, m_a4)
        m_t2 = run("tensor", lambda tt: tt.transpose(
            psM[0:4, 4:132], ua4[:, :], ident_f[:, :]), tP)
        wait("vector", tP, m_t2)
        alf_m[t] = run("vector", lambda v, t=t: v.tensor_copy(
            alfb[:, t * 128:(t + 1) * 128], psM[0:4, 4:132]), vP)

    wait("tensor", vP, alf_m[NT - 1])
    def t_ae(t):
        t.matmul(psE[:, 0:512], sel16b[:, :], alfb[:, 0:512],
                 start=True, stop=True)
        return t.matmul(psE[:, 512:1024], sel16b[:, :], alfb[:, 512:1024],
                        start=True, stop=True)
    m_ae = run("tensor", t_ae, tP)
    wait("vector", tP, m_ae)
    m_aes = run("vector", lambda v: v.tensor_copy(ae_s[:, :], psE[:, :]), vP)
    wait("vector", vP, m_aes)
    wait("vector", tP, m_z)
    m_zsc = run("vector", lambda v: v.scalar_tensor_tensor(
        zscB[:, :], psZ[:, :], 2.0, ae_s[:, :], ALU.mult, ALU.mult), vP)

    free("sel16b", "ae_s", "rc4", "nm4", "al4", "ua4", "zscB", "alfb", "alf", "lga", "lgf")
    wbb = [sbuf("wbb%d" % i, [128, 4096], BF16, side="right") for i in range(4)]
    ff_fm = [sbuf("ffa", [128, 1024], BF16), sbuf("ffb", [128, 1024], BF16)]
    ff_t = [sbuf("fta", [128, 1024], F32), sbuf("ftb", [128, 1024], F32)]

    wb_m = {}
    wait("gpsimd", tP, m_p5_t)
    wait("gpsimd", vP, nrm5_m[NT - 1])
    wait("gpsimd", dO[0], dO_all[0])
    wait("gpsimd", dO[1], dO_all[1])
    for oc in range(4):
        run("gpsimd", lambda g, oc=oc: g.dma_start(
            wbb[oc % 4][:, :], wb_d[oc]), dW4[oc % 4], 16)
        wb_m[oc] = dW4[oc % 4].n

    free("psE", "psM", "psL", "psZ")
    psF = [psum("psFa", [128, 1024], F32), psum("psFb", [128, 1024], F32)]
    psX = [psum("psXa", [128, 1024], BF16), psum("psXb", [128, 1024], BF16)]

    f_mm = {}; f_ev1 = {}; f_tt = {}; f_ev2 = {}; f_dma = {}
    for oc in range(32):
        wait("tensor", dW4[oc % 4], wb_m[oc])
        wait("tensor", vP, f_ev1[oc - 2] if oc >= 2 else m_zsc)
        if oc == 0:
            wait("tensor", dWb, m_pb)
            wait("vector", dO[0], dO_all[0])
            wait("vector", dO[1], dO_all[1])
        def t_f(t, oc=oc):
            par = oc % 2
            for hcc in range(HC):
                for hf in range(2):
                    t.matmul(psF[par][:, hf * 512:(hf + 1) * 512],
                             wbb[oc % 4][:, hcc * 128:(hcc + 1) * 128],
                             u2fm[:, hcc * 1024 + hf * 512:
                                  hcc * 1024 + (hf + 1) * 512],
                             start=(hcc == 0), stop=False)
            last = None
            for hf in range(2):
                last = t.matmul(psF[par][:, hf * 512:(hf + 1) * 512],
                                lball[:, oc * 128:(oc + 1) * 128],
                                zscB[:, hf * 512:(hf + 1) * 512],
                                start=False, stop=True)
            return last
        f_mm[oc] = run("tensor", t_f, tP)
        if oc + 4 < 32:
            wait("gpsimd", tP, f_mm[oc])
            run("gpsimd", lambda g, oc=oc: g.dma_start(
                wbb[(oc + 4) % 4][:, :], wb_d[oc + 4]), dW4[(oc + 4) % 4], 16)
            wb_m[oc + 4] = dW4[(oc + 4) % 4].n
        wait("vector", tP, f_mm[oc])
        f_ev1[oc] = run("vector", lambda v, oc=oc: v.tensor_scalar_add(
            ff_fm[oc % 2][:, :], psF[oc % 2][:, :], bb_s[:, oc:oc + 1]), vP)
        wait("tensor", vP, f_ev1[oc])
        if oc >= 2:
            wait("tensor", vP, f_ev2[oc - 2])
        def t_ft(t, oc=oc):
            last = None
            for j in range(8):
                last = t.transpose(psX[oc % 2][:, j * 128:(j + 1) * 128],
                                   ff_fm[oc % 2][:, j * 128:(j + 1) * 128],
                                   ident_b[:, :])
            return last
        f_tt[oc] = run("tensor", t_ft, tP)
        wait("vector", tP, f_tt[oc])
        if oc >= 2:
            wait("vector", dA[oc % 2], f_dma[oc - 2])
        f_ev2[oc] = run("vector", lambda v, oc=oc: v.tensor_copy(
            ff_t[oc % 2][:, :], psX[oc % 2][:, :]), vP)
        wait("gpsimd", vP, f_ev2[oc])
        if oc == 0:
            wait("gpsimd", dO[0], dO_all[0])
            wait("gpsimd", dO[1], dO_all[1])
        def g_acc(g, oc=oc):
            o = bass.AP(out_h, oc * 128,
                        [[4096, 128], [128 * 4096, 8], [1, 128]])
            if _ACCUM:
                return g.dma_start(o, ff_t[oc % 2][:, :], accum_op=ALU.add)
            return g.dma_start(o, ff_t[oc % 2][:, :])
        run("gpsimd", g_acc, dA[oc % 2], 16)
        f_dma[oc] = dA[oc % 2].n

    wait("gpsimd", dO[0], dO_all[0])
    wait("gpsimd", dO[1], dO_all[1])
    wait("gpsimd", dA[0], dA[0].n)
    wait("gpsimd", dA[1], dA[1].n)

    block = es.enter_context(nc.Block())
    for eng in ENGS:
        def _runner(e, fns=tuple(prog[eng])):
            for f in fns:
                f(e)
        getattr(block, eng)(_runner)

    for name in reversed(list(scopes)):
        scopes.pop(name).close()
    es.close()
    return nc


def _prep(inputs):
    f = np.ascontiguousarray
    x = inputs["L0_attn_inputs"]
    conv_w = inputs["L0_attn_conv_w"]
    conv_b = inputs["L0_attn_conv_b"]
    wq = inputs["L0_attn_wq"].reshape(1024, 1024)
    bq = inputs["L0_attn_bq"].reshape(1024)
    wk = inputs["L0_attn_wk"].reshape(1024, 1024)
    bk = inputs["L0_attn_bk"].reshape(1024)
    wv = inputs["L0_attn_wv"].reshape(1024, 1024)
    bv = inputs["L0_attn_bv"].reshape(1024)
    wo = inputs["L0_attn_wo"].reshape(1024, 1024)
    bo = inputs["L0_attn_bo"].reshape(1024)
    pw = inputs["L0_attn_proj_w"]
    pb = inputs["L0_attn_proj_b"]
    wb = inputs["L0_ffn_W_base"]
    bb = inputs["L0_ffn_b_base"]
    la = inputs["L0_ffn_lora_A"]
    lb = inputs["L0_ffn_lora_B"]
    rw = inputs["L0_ffn_router_W"]
    rb = inputs["L0_ffn_router_b"]
    pr = inputs["L0_ffn_domain_prior"]
    ac = inputs["L0_ffn_active"]

    com = {
        "cw": f(np.asarray(conv_w).reshape(3, 32, 128, 24, 128)
                .transpose(3, 2, 0, 1, 4).reshape(24, 128, 12288)),
        "cb": f(np.asarray(conv_b).reshape(24, 128).T),
        "wq": f(np.asarray(wq).reshape(8, 128, 8, 128)
                .transpose(1, 0, 2, 3).reshape(128, 8192)),
        "wk": f(np.asarray(wk).reshape(8, 128, 8, 128)
                .transpose(1, 0, 2, 3).reshape(128, 8192)),
        "wv": f(np.asarray(wv).reshape(8, 128, 1024)
                .transpose(1, 0, 2).reshape(128, 8192)),
        "wo": f(np.asarray(wo).reshape(8, 128, 8, 128)
                .transpose(1, 0, 2, 3).reshape(128, 8192)),
        "bq": f(np.asarray(bq).reshape(8, 128).T),
        "bk": f(np.asarray(bk).reshape(8, 128).T),
        "bo": f(np.asarray(bo).reshape(8, 128).T),
        "bv": f(np.asarray(bv).reshape(8, 128).T),
        "pw": f(np.asarray(pw).reshape(8, 128, 4096)
                .transpose(1, 0, 2).reshape(128, 32768)),
        "pb": f(np.asarray(pb).reshape(1, 4096)),
        "wb": f(np.asarray(wb).T.reshape(32, 128, 32, 128)
                .transpose(2, 1, 0, 3).reshape(32, 128, 4096)),
        "bb": f(np.asarray(bb).reshape(32, 128).T),
        "la": f(np.asarray(la).transpose(2, 0, 1).reshape(4096, 128)
                .reshape(32, 128, 128).transpose(1, 0, 2).reshape(128, 4096)),
        "lb": f(np.asarray(lb).transpose(0, 2, 1).reshape(128, 4096)),
        "rw": f(np.asarray(rw).T.reshape(32, 128, 16)
                .transpose(1, 0, 2).reshape(128, 512)),
        "rb": f(np.asarray(rb).reshape(16, 1)),
        "pr": f(np.asarray(pr).reshape(16, 1)),
        "ac": f(np.asarray(ac).reshape(1, 4)),
    }

    x = np.asarray(x)
    in_maps = []
    for c in range(8):
        lo, hi = c * 512, (c + 1) * 512
        xs = f(np.concatenate([x[0, lo:hi, :], x[1, lo:hi, :]], axis=0))
        halo = np.zeros((4, 4096), dtype=np.float32)
        if lo > 0:
            halo[0] = x[0, lo - 1, :]
            halo[2] = x[1, lo - 1, :]
        if hi < 4096:
            halo[1] = x[0, hi, :]
            halo[3] = x[1, hi, :]
        m = {"x": xs, "xh": f(halo)}
        m.update(com)
        in_maps.append(m)
    return in_maps


def kernel(**inputs):
    in_maps = _prep(inputs)
    nc = build_program()
    res = run_bass_kernel_spmd(nc, in_maps, core_ids=list(range(8)))
    out = np.empty((2, 4096, 4096), dtype=np.float32)
    for c in range(8):
        r = res.results[c]["out"]
        out[0, c * 512:(c + 1) * 512, :] = r[0:512]
        out[1, c * 512:(c + 1) * 512, :] = r[512:1024]
    return out


# version 16
# speedup vs baseline: 1.0179x; 1.0044x over previous
import contextlib
import numpy as np

import concourse.bass as bass
import concourse.mybir as mybir
from concourse.bass_utils import run_bass_kernel_spmd

F32 = mybir.dt.float32
BF16 = mybir.dt.bfloat16
I32 = mybir.dt.int32
ALU = mybir.AluOpType
AF = mybir.ActivationFunctionType
AX = mybir.AxisListType

D = 4096
HC = 32
OCC = 24
T = 1024
NT = 8
NEG = -1e9
EPS_LN = 1e-5
EPS_PRIOR = 1e-12

ENGS = ("sync", "gpsimd", "tensor", "vector", "scalar")

_STOP = None
_P3S = None
_ACCUM = True


def build_program():
    nc = bass.Bass("TRN2", target_bir_lowering=False)

    def din(name, shape, dtype=F32):
        return nc.dram_tensor(name, shape, dtype, kind="ExternalInput").ap()

    x_d = din("x", [T, D])
    xh_d = din("xh", [4, D])
    cw_d = din("cw", [OCC, 128, 12288])
    cb_d = din("cb", [128, OCC])
    wq_d = din("wq", [128, 8192])
    wk_d = din("wk", [128, 8192])
    wv_d = din("wv", [128, 8192])
    wo_d = din("wo", [128, 8192])
    bq_d = din("bq", [128, 8])
    bk_d = din("bk", [128, 8])
    bo_d = din("bo", [128, 8])
    bv_d = din("bv", [128, 8])
    pw_d = din("pw", [128, 32768])
    pb_d = din("pb", [1, 4096])
    wb_d = din("wb", [HC, 128, 4096])
    bb_d = din("bb", [128, 32])
    la_d = din("la", [128, 4096])
    lb_d = din("lb", [128, 4096])
    rw_d = din("rw", [128, 512])
    rb_d = din("rb", [16, 1])
    pr_d = din("pr", [16, 1])
    ac_d = din("ac", [1, 4], I32)

    out_h = nc.dram_tensor("out", [T, D], F32, kind="ExternalOutput")
    out_d = out_h.ap()

    es = contextlib.ExitStack()
    scopes = {}

    def sbuf(name, shape, dtype, side="left"):
        s = contextlib.ExitStack()
        scopes[name] = s
        return s.enter_context(nc.sbuf_tensor(name, shape, dtype, side=side))

    def psum(name, shape, dtype=F32):
        s = contextlib.ExitStack()
        scopes[name] = s
        return s.enter_context(nc.psum_tensor(name, shape, dtype))

    def free(*names):
        for n in names:
            scopes.pop(n).close()

    class Sem:
        def __init__(self, name):
            self.h = es.enter_context(nc.semaphore(name))
            self.n = 0

    dS = Sem("dS")
    dX = [Sem("dXa"), Sem("dXb")]
    dWa = Sem("dWa")
    dWb = Sem("dWb")
    dWc = Sem("dWc")
    dW2 = Sem("dW2")
    dW2b = Sem("dW2b")
    dW3 = Sem("dW3")
    dWq = [Sem("dWq0"), Sem("dWq1")]
    dX5 = [Sem("dX5a"), Sem("dX5b")]
    dW4 = [Sem("dW4a"), Sem("dW4b"), Sem("dW4c"), Sem("dW4d")]
    dO = [Sem("dOa"), Sem("dOb")]
    dA = [Sem("dAa"), Sem("dAb")]
    tP = Sem("tP"); vP = Sem("vP"); sP = Sem("sP"); gP = Sem("gP")

    prog = {e: [] for e in ENGS}

    def em(eng, fn):
        prog[eng].append(fn)

    def wait(eng, sem, val):
        em(eng, lambda e, h=sem.h, v=val: e.wait_ge(h, v))

    CHAIN = {"vector": vP, "scalar": sP, "gpsimd": gP}
    chain_val = {"vector": 0, "scalar": 0, "gpsimd": 0}

    def run(eng, fn, sem=None, k=1, chain=True):
        if chain and eng in CHAIN and chain_val[eng] > 0:
            wait(eng, CHAIN[eng], chain_val[eng])
        if sem is None:
            em(eng, fn)
            return None
        sem.n += k
        em(eng, lambda e, f=fn, h=sem.h, kk=k: f(e).then_inc(h, kk))
        if eng in CHAIN and sem is CHAIN[eng]:
            chain_val[eng] = sem.n
        return sem.n

    ident_b = sbuf("ident_b", [128, 128], BF16)
    ident_f = sbuf("ident_f", [128, 128], F32)
    onesT = sbuf("onesT", [128, 128], BF16)
    ones_f = sbuf("ones_f", [1, 128], F32)
    cst = sbuf("cst", [128, 4], F32)
    bq_s = sbuf("bq_s", [128, 8], F32)
    bk_s = sbuf("bk_s", [128, 8], F32)
    bo_s = sbuf("bo_s", [128, 8], F32)
    bv_s = sbuf("bv_s", [128, 8], F32)
    cb_s = sbuf("cb_s", [128, OCC], F32)
    bb_s = sbuf("bb_s", [128, 32], F32)
    st2 = sbuf("st2", [128, 4], F32)
    rst = sbuf("rst", [128, 1], F32)
    LBP = sbuf("LBP", [128, 4096], BF16)
    ctile = sbuf("ctile", [128, 4], F32)
    exp16 = sbuf("exp16", [16, 128], F32)
    sel16 = sbuf("sel16", [4, 128], F32)
    ohb = sbuf("ohb", [16, 4], BF16)
    ohf = sbuf("ohf", [16, 4], F32)
    iot = sbuf("iot", [16, 1], F32)
    actf = sbuf("actf", [1, 4], F32)
    acts = sbuf("acts", [1, 4], I32)
    pr_s = sbuf("pr_s", [16, 1], F32)
    rb_s = sbuf("rb_s", [16, 1], F32)
    c16 = sbuf("c16", [16, 1], F32)
    csel = sbuf("csel", [1, 4], F32)
    trash2 = sbuf("trash2", [128, D], BF16)

    u1fm = sbuf("u1fm", [128, HC * 1028], BF16)
    cwq = [sbuf("cwq0", [128, 3072], BF16), sbuf("cwq1", [128, 3072], BF16)]
    xf = [sbuf("xf0", [128, D], F32), sbuf("xf1", [128, D], F32)]
    u1t = [sbuf("u1t0", [128, D], BF16), sbuf("u1t1", [128, D], BF16)]
    trash = sbuf("trash", [128, D], BF16)
    ht = sbuf("ht", [4, D], F32)
    hst = sbuf("hst", [4, 4], F32)
    hrst = sbuf("hrst", [4, 1], F32)
    u1h = sbuf("u1h", [4, D], BF16)

    psT = [psum("psT0", [128, 1024], BF16), psum("psT1", [128, 1024], BF16)]
    psS = psum("psS", [128, 512], F32)

    xpre_m = []
    for tt0 in range(2):
        run("sync", lambda q, tt0=tt0: q.dma_start(
            out=xf[tt0][:, :], in_=x_d[tt0 * 128:(tt0 + 1) * 128, :]),
            dX[tt0], 16)
        xpre_m.append(dX[tt0].n)
    for dst, src in ((ht, xh_d), (acts, ac_d), (pr_s, pr_d), (rb_s, rb_d),
                     (bq_s, bq_d), (bk_s, bk_d), (bo_s, bo_d), (bv_s, bv_d),
                     (cb_s, cb_d), (bb_s, bb_d)):
        run("sync", lambda q, a=dst, b=src: q.dma_start(out=a[:, :], in_=b),
            dS, 16)
    m_in_setup = dS.n

    run("gpsimd", lambda g: g.memset(ident_b[:, :], 1.0), gP)
    run("gpsimd", lambda g: g.affine_select(
        ident_b[:, :], ident_b[:, :], [[-1, 128]], ALU.is_equal,
        0.0, base=0, channel_multiplier=1), gP)
    run("gpsimd", lambda g: g.memset(ident_f[:, :], 1.0), gP)
    run("gpsimd", lambda g: g.affine_select(
        ident_f[:, :], ident_f[:, :], [[-1, 128]], ALU.is_equal,
        0.0, base=0, channel_multiplier=1), gP)
    def g_c1(g):
        g.memset(onesT[:, :], 1.0)
        g.memset(ones_f[:, :], 1.0)
        g.memset(cst[:, 0:1], EPS_LN)
        g.memset(cst[:, 1:2], -0.5)
        g.memset(cst[:, 2:3], EPS_PRIOR)
        g.memset(cst[:, 3:4], 1.0 / D)
        g.iota(iot[:, :], [[0, 1]], base=0, channel_multiplier=1,
               allow_small_or_imprecise_dtypes=True)
        return g.memset(exp16[:, :], 1.0)
    run("gpsimd", g_c1, gP)
    run("gpsimd", lambda g: g.affine_select(
        exp16[:, :], exp16[:, :], [[1, 128]], ALU.is_ge, 0.0,
        base=0, channel_multiplier=-8), gP)
    m_g_const = run("gpsimd", lambda g: g.affine_select(
        exp16[:, :], exp16[:, :], [[-1, 128]], ALU.is_ge, 0.0,
        base=7, channel_multiplier=8), gP)

    run("gpsimd", lambda g: g.dma_start(LBP[32:33, 0:4096], pb_d), dWb, 16)
    m_pb = dWb.n

    def ln_stats(xin, stats, rstd, nparts, vwait=None):
        ms = run("scalar", lambda s: s.activation(
            trash[0:nparts, :], xin, AF.Square, accum_out=stats[:, 1:2]), sP)
        if vwait is not None:
            wait("vector", vwait[0], vwait[1])
        run("vector", lambda v, nparts=nparts: v.tensor_scalar(
            trash2[0:nparts, :], xin, 1.0, 0.0, ALU.mult, ALU.add,
            accum_out=stats[:, 0:1]), vP)
        wait("vector", sP, ms)
        m1 = run("vector", lambda v: v.tensor_scalar_mul(
            stats[:, 2:3], stats[:, 0:1], 1.0 / D), vP)
        wait("vector", vP, m1)
        m2 = run("vector", lambda v: v.tensor_scalar_mul(
            stats[:, 0:1], stats[:, 1:2], 1.0 / D), vP)
        wait("vector", vP, m2)
        m3 = run("vector", lambda v: v.tensor_tensor(
            stats[:, 1:2], stats[:, 2:3], stats[:, 2:3], ALU.mult), vP)
        wait("vector", vP, m3)
        m4 = run("vector", lambda v: v.tensor_sub(
            stats[:, 3:4], stats[:, 0:1], stats[:, 1:2]), vP)
        wait("scalar", vP, m4)
        run("scalar", lambda s: s.activation(
            rstd[:, :], stats[0:nparts, 3:4], AF.Ln,
            bias=cst[0:nparts, 0:1]), sP)
        return run("scalar", lambda s: s.activation(
            rstd[:, :], rstd[:, :], AF.Exp, scale=cst[0:nparts, 1:2]), sP)

    norm_m = {}
    norm_s = {}
    tev_m = {}
    xin_m = {}

    ms0_m = {}
    vs0_m = {}
    vs5_m = {}

    def emit_ln0a(t):
        wait("scalar", dX[t % 2], xin_m[t])
        ms0_m[t] = run("scalar", lambda s, t=t: s.activation(
            trash[:, :], xf[t % 2][:, :], AF.Square,
            accum_out=st2[:, 1:2]), sP)
        wait("vector", dX[t % 2], xin_m[t])
        vs0_m[t] = run("vector", lambda v, t=t: v.tensor_scalar(
            trash2[:, :], xf[t % 2][:, :], 1.0, 0.0, ALU.mult, ALU.add,
            accum_out=st2[:, 0:1]), vP)

    def emit_ln0b(t):
        wait("vector", sP, ms0_m[t])
        wait("scalar", vP, vs0_m[t])
        run("scalar", lambda s: s.activation(
            st2[:, 2:3], st2[:, 0:1], AF.Identity, scale=cst[:, 3:4]), sP)
        run("scalar", lambda s: s.activation(
            st2[:, 0:1], st2[:, 2:3], AF.Square), sP)
        mq = run("scalar", lambda s: s.activation(
            st2[:, 1:2], st2[:, 1:2], AF.Identity, scale=cst[:, 3:4],
            bias=cst[:, 0:1]), sP)
        wait("vector", sP, mq)
        m4 = run("vector", lambda v: v.tensor_sub(
            st2[:, 3:4], st2[:, 1:2], st2[:, 0:1]), vP)
        wait("scalar", vP, m4)
        run("scalar", lambda s: s.activation(
            rst[:, :], st2[:, 3:4], AF.Ln), sP)
        mr = run("scalar", lambda s: s.activation(
            rst[:, :], rst[:, :], AF.Exp, scale=cst[:, 1:2]), sP)
        wait("vector", sP, mr)
        mb = run("vector", lambda v: v.tensor_scalar(
            st2[:, 0:1], st2[:, 2:3], rst[:, 0:1], -1.0,
            ALU.mult, ALU.mult), vP)
        norm_m[t] = run("vector", lambda v, t=t: v.tensor_scalar(
            u1t[t % 2][:, 0:2048], xf[t % 2][:, 0:2048],
            st2[:, 2:3], rst[:, 0:1], ALU.subtract, ALU.mult), vP)
        wait("scalar", vP, mb)
        norm_s[t] = run("scalar", lambda s, t=t: s.activation(
            u1t[t % 2][:, 2048:4096], xf[t % 2][:, 2048:4096], AF.Identity,
            scale=rst[:, 0:1], bias=st2[:, 0:1]), sP)

    def emit_ln0(t):
        emit_ln0a(t)
        emit_ln0b(t)

    def emit_xdma(t):
        if t >= 2:
            wait("sync", vP, norm_m[t - 2])
            wait("sync", sP, norm_s[t - 2])
        run("sync", lambda q, t=t: q.dma_start(
            out=xf[t % 2][:, :], in_=x_d[t * 128:(t + 1) * 128, :]),
            dX[t % 2], 16)
        xin_m[t] = dX[t % 2].n

    wait("scalar", gP, m_g_const)
    wait("vector", gP, m_g_const)
    gidx = 0
    for t in range(NT):
        if t == 0:
            xin_m[0], xin_m[1] = xpre_m
            emit_ln0(0)
        if t + 2 < NT:
            emit_xdma(t + 2)

        scol = (t // 4) * 514 + 1 + (t % 4) * 128
        for grp in range(4):
            if grp < 2:
                wait("tensor", vP, norm_m[t])
            else:
                wait("tensor", sP, norm_s[t])
            if gidx >= 2:
                wait("tensor", vP, tev_m[gidx - 2])
            def t_grp(tt, t=t, grp=grp, par=gidx % 2):
                last = None
                for j in range(8):
                    hcc = grp * 8 + j
                    last = tt.transpose(psT[par][:, j * 128:(j + 1) * 128],
                                        u1t[t % 2][:, hcc * 128:(hcc + 1) * 128],
                                        ident_b[:, :])
                return last
            mT = run("tensor", t_grp, tP)
            wait("vector", tP, mT)
            def v_grp(v, grp=grp, scol=scol, par=gidx % 2):
                o = bass.AP(u1fm, (grp * 8) * 1028 + scol,
                            [[HC * 1028, 128], [1028, 8], [1, 128]])
                i = bass.AP(psT[par], 0, [[1024, 128], [128, 8], [1, 128]])
                return v.tensor_copy(o, i)
            tev_m[gidx] = run("vector", v_grp, vP)
            gidx += 1
            if grp == 1 and t + 1 < NT:
                emit_ln0a(t + 1)
            if grp == 3 and t + 1 < NT:
                emit_ln0b(t + 1)

    wait("scalar", dS, m_in_setup)
    mr = ln_stats(ht[:, :], hst, hrst, 4, vwait=(dS, m_in_setup))
    wait("vector", sP, mr)
    m_u1h = run("vector", lambda v: v.tensor_scalar(
        u1h[:, :], ht[:, :], hst[:, 2:3], hrst[:, 0:1],
        ALU.subtract, ALU.mult), vP)
    wait("tensor", vP, m_u1h)
    wait("tensor", gP, m_g_const)
    wait("tensor", vP, tev_m[gidx - 1])
    wait("tensor", vP, tev_m[gidx - 2])
    def t_halo(t):
        last = None
        for hcc in range(HC):
            last = t.transpose(psT[0][:, hcc * 4: hcc * 4 + 4],
                               u1h[:, hcc * 128:(hcc + 1) * 128],
                               ident_b[0:4, 0:4])
        return last
    m_haloT = run("tensor", t_halo, tP)
    wait("vector", tP, m_haloT)
    def v_halo_evac(v):
        o0 = bass.AP(u1fm, 0, [[HC * 1028, 128], [1028, HC], [513, 2]])
        i0 = bass.AP(psT[0], 0, [[1024, 128], [4, HC], [1, 2]])
        v.tensor_copy(o0, i0)
        o1 = bass.AP(u1fm, 514, [[HC * 1028, 128], [1028, HC], [513, 2]])
        i1 = bass.AP(psT[0], 2, [[1024, 128], [4, HC], [1, 2]])
        return v.tensor_copy(o1, i1)
    m_halo_ev = run("vector", v_halo_evac, vP)

    wait("vector", dS, m_in_setup)
    m_actf = run("vector", lambda v: v.tensor_copy(actf[:, :], acts[:, :]), vP)
    wait("tensor", vP, m_actf)
    wait("tensor", gP, m_g_const)
    m_ohp = run("tensor", lambda t: t.matmul(
        psS[0:16, 0:4], ones_f[0:1, 0:16], actf[:, :], start=True, stop=True),
        tP)
    wait("vector", tP, m_ohp)
    run("vector", lambda v: v.tensor_tensor(
        ohf[:, :], psS[0:16, 0:4], iot[:, :].broadcast_to([16, 4]),
        ALU.is_equal), vP)
    m_oh = run("vector", lambda v: v.tensor_copy(ohb[:, :], ohf[:, :]), vP)

    wait("scalar", dS, m_in_setup)
    wait("scalar", gP, m_g_const)
    m_lnp = run("scalar", lambda s: s.activation(
        c16[:, :], pr_s[:, :], AF.Ln, bias=cst[0:16, 2:3]), sP)
    wait("vector", sP, m_lnp)
    m_c16 = run("vector", lambda v: v.tensor_add(
        c16[:, :], c16[:, :], rb_s[:, :]), vP)
    wait("tensor", vP, m_c16)
    m_cs = run("tensor", lambda t: t.matmul(
        psS[0:1, 4:8], c16[:, :], ohf[:, :], start=True, stop=True), tP)
    wait("vector", tP, m_cs)
    m_cse = run("vector", lambda v: v.tensor_copy(csel[:, :], psS[0:1, 4:8]),
                vP)
    wait("tensor", vP, m_cse)
    m_ct = run("tensor", lambda t: t.matmul(
        psS[:, 8:12], ones_f[:, :], csel[:, :], start=True, stop=True), tP)
    wait("vector", tP, m_ct)
    m_ctile = run("vector", lambda v: v.tensor_copy(ctile[:, :], psS[:, 8:12]),
                  vP)

    wait("tensor", vP, m_oh)
    wait("tensor", vP, m_ctile)
    m_s16 = run("tensor", lambda t: t.matmul(
        psS[0:4, 0:128], ohf[:, :], exp16[:, :], start=True, stop=True), tP)
    wait("vector", tP, m_s16)
    m_sel16 = run("vector", lambda v: v.tensor_copy(
        sel16[:, :], psS[0:4, 0:128]), vP)

    m_p0_done_v = vP.n
    m_p0_done_t = tP.n

    def _finish_early(sem, val):
        wait("sync", sem, val)
        run("sync", lambda q: q.dma_start(out_d[0:128, 0:128],
                                          ident_f[:, 0:128]), dO[0], 16)
        wait("sync", dO[0], dO[0].n)
        block = es.enter_context(nc.Block())
        for eng in ENGS:
            def _runner(e, fns=tuple(prog[eng])):
                for f in fns:
                    f(e)
            getattr(block, eng)(_runner)
        for name in reversed(list(scopes)):
            scopes.pop(name).close()
        es.close()
        return nc

    if _STOP == "P0":
        return _finish_early(vP, m_p0_done_v)

    free("u1h", "hrst", "hst", "ht",
         "trash", "u1t1", "u1t0", "xf1", "xf0")

    qkvfm = sbuf("qkvfm", [128, OCC * 1024], BF16, side="right")
    wq_s = sbuf("wq_s", [128, 8192], BF16, side="right")
    wk_s = sbuf("wk_s", [128, 8192], BF16, side="right")
    psC = [psum("psC0", [128, 1024], F32), psum("psC1", [128, 1024], F32)]

    qdma_m = {}
    qte_m = {}
    cev_m = {}
    for oc in range(OCC):
        for q in range(4):
            k = oc * 4 + q
            if k in (48, 52, 56, 60):
                if k == 48:
                    wait("gpsimd", tP, m_p0_done_t)
                i4 = (k - 48) // 4
                wsrc = (wq_d, wq_d, wk_d, wk_d)[i4]
                wdst = (wq_s, wq_s, wk_s, wk_s)[i4]
                hlf = (i4 % 2) * 4096
                run("gpsimd", lambda g, wsrc=wsrc, wdst=wdst, hlf=hlf:
                    g.dma_start(wdst[:, hlf:hlf + 4096],
                                wsrc[:, hlf:hlf + 4096]), dW2, 16)
            if k >= 2:
                wait("gpsimd", tP, qte_m[k - 2])
            run("gpsimd", lambda g, oc=oc, q=q, k=k: g.dma_start(
                cwq[k % 2][:, :], cw_d[oc, :, q * 3072:(q + 1) * 3072]),
                dWq[k % 2], 16)
            qdma_m[k] = dWq[k % 2].n
            wait("tensor", dWq[k % 2], qdma_m[k])
            if q == 0:
                wait("tensor", vP, cev_m[oc - 2] if oc >= 2 else m_p0_done_v)
            def t_conv(t, q=q, k=k, oc=oc):
                last = None
                for gi in range(24):
                    g = q * 24 + gi
                    w, hcc = g // HC, g % HC
                    for seg in range(2):
                        base = hcc * 1028 + seg * 514 + w
                        last = t.matmul(
                            psC[oc % 2][:, seg * 512:(seg + 1) * 512],
                            cwq[k % 2][:, gi * 128:(gi + 1) * 128],
                            u1fm[:, base: base + 512],
                            start=(g == 0), stop=(g == 95))
                return last
            qte_m[k] = run("tensor", t_conv, tP)
        wait("vector", tP, qte_m[oc * 4 + 3])
        cev_m[oc] = run("vector", lambda v, oc=oc: v.tensor_scalar_add(
            qkvfm[:, oc * 1024:(oc + 1) * 1024], psC[oc % 2][:, :],
            cb_s[:, oc:oc + 1]), vP)

    m_conv_t = tP.n
    m_conv_v = vP.n
    if _STOP == "P1":
        return _finish_early(vP, m_conv_v)

    free("cwq1", "cwq0", "u1fm")
    wo_s = sbuf("wo_s", [128, 8192], BF16)
    qh_s = sbuf("qh_s", [128, 8192], BF16)
    kh_s = sbuf("kh_s", [128, 8192], BF16)
    vh_s = sbuf("vh_s", [128, 8192], BF16)
    o_fm = sbuf("o_fm", [128, 8192], BF16)
    wv_s = sbuf("wv_s", [128, 8192], BF16)

    wait("gpsimd", tP, m_conv_t)
    run("gpsimd", lambda g: g.dma_start(wv_s[:, :], wv_d), dW2b, 16)
    run("gpsimd", lambda g: g.dma_start(wo_s[:, :], wo_d), dW2b, 16)
    m_wq = m_wk = 64
    m_wv = m_wo = 32

    p2ev = {}
    p2i = 0
    for kind in range(3):
        wm = (m_wq, m_wk, m_wv)[kind]
        ws = (dW2, dW2, dW2b)[kind]
        for m in range(8):
            wait("tensor", ws, wm)
            wait("tensor", vP, p2ev[p2i - 2] if p2i >= 2 else m_conv_v)
            def t_p2(t, kind=kind, m=m, par=p2i % 2):
                last = None
                wsrc = (wq_s, wk_s, wv_s)[kind]
                for k in range(8):
                    for hf in range(2):
                        if kind < 2:
                            lhs = wsrc[:, (k * 8 + m) * 128:
                                       (k * 8 + m + 1) * 128]
                            rhs = qkvfm[:, (kind * 8 + k) * 1024 + hf * 512:
                                        (kind * 8 + k) * 1024 + (hf + 1) * 512]
                        else:
                            lhs = qkvfm[:, (16 + k) * 1024 + m * 128:
                                        (16 + k) * 1024 + (m + 1) * 128]
                            rhs = wsrc[:, k * 1024 + hf * 512:
                                       k * 1024 + (hf + 1) * 512]
                        last = t.matmul(
                            psC[par][:, hf * 512:(hf + 1) * 512],
                            lhs, rhs, start=(k == 0), stop=(k == 7))
                return last
            mte = run("tensor", t_p2, tP)
            wait("vector", tP, mte)
            if kind == 0:
                fn = (lambda v, m=m, par=p2i % 2: v.tensor_scalar(
                    qh_s[:, m * 1024:(m + 1) * 1024], psC[par][:, :],
                    bq_s[:, m:m + 1], 0.125, ALU.add, ALU.mult))
            elif kind == 1:
                fn = (lambda v, m=m, par=p2i % 2: v.tensor_scalar_add(
                    kh_s[:, m * 1024:(m + 1) * 1024], psC[par][:, :],
                    bk_s[:, m:m + 1]))
            else:
                fn = (lambda v, m=m, par=p2i % 2: v.tensor_copy(
                    vh_s[:, m * 1024:(m + 1) * 1024], psC[par][:, :]))
            p2ev[p2i] = run("vector", fn, vP)
            p2i += 1
    m_p2_t = tP.n
    m_p2_v = vP.n
    if _STOP == "P2":
        return _finish_early(vP, m_p2_v)

    free("wv_s")
    free("wk_s", "wq_s", "qkvfm")
    free("psC1", "psC0", "psS", "psT1", "psT0")
    mask16 = sbuf("mask16", [128, 2048], F32)
    sm_s = sbuf("sm_s", [128, 2048], F32)
    ss_s = sbuf("ss_s", [128, 2048], F32)
    p32_s = sbuf("p32_s", [128, 2048], F32)
    pbf_s = sbuf("pbf_s", [128, 2048], BF16)
    pT_s = sbuf("pT_s", [128, 2048], BF16)
    nm16 = sbuf("nm16", [128, 16], F32)
    rs16 = sbuf("rs16", [128, 16], F32)
    ri16 = sbuf("ri16", [128, 16], F32)
    pw_s = sbuf("pw_s", [128, 32768], BF16, side="right")

    psA = psum("psA", [128, 2048], F32)
    psPT = psum("psPT", [128, 2048], BF16)
    psO = psum("psO", [128, 1024], F32)

    wait("gpsimd", tP, m_p2_t)
    run("gpsimd", lambda g: g.memset(mask16[:, :], 0.0), gP)
    m_mask = run("gpsimd", lambda g: g.affine_select(
        mask16[:, :], mask16[:, :], [[0, 16], [-1, 128]],
        ALU.is_ge, NEG, base=0, channel_multiplier=1), gP)
    for i in range(4):
        run("gpsimd", lambda g, i=i: g.dma_start(
            pw_s[:, i * 8192:(i + 1) * 8192],
            pw_d[:, i * 8192:(i + 1) * 8192]), dW3, 16)
    m_pw = dW3.n

    smv = {}; pbfv = {}; ptev = {}; oev = {}; expv = {}
    ptm = {}; pvm = {}; scm = {}

    def emit_sc(b):
        if b == 0:
            wait("tensor", vP, m_p2_v)
        else:
            wait("tensor", vP, smv[(b - 1, 1)])
        def t_sc(t, b=b):
            last = None
            for h in list(range(0, 16, 2)) + list(range(1, 16, 2)):
                r0 = 64 * (h % 2)
                cc = (h // 2) * 1024 + b * 128
                last = t.matmul(psA[:, h * 128:(h + 1) * 128],
                                qh_s[r0:r0 + 64, cc:cc + 128],
                                kh_s[r0:r0 + 64, cc:cc + 128],
                                start=True, stop=True)
            return last
        scm[b] = run("tensor", t_sc, tP)

    for b in range(NT):
        if b == 0:
            emit_sc(0)
        m_sc = scm[b]

        for half in range(2):
            c0 = half * 1024
            h0 = half * 8
            wait("vector", tP, m_sc)
            if b == 0 and half == 0:
                wait("vector", gP, m_mask)
            if b > 0:
                wait("vector", sP, expv[(b - 1, half)])
            smv[(b, half)] = run("vector", lambda v, c0=c0: v.tensor_add(
                sm_s[:, c0:c0 + 1024], psA[:, c0:c0 + 1024],
                mask16[:, c0:c0 + 1024]), vP)
            run("vector", lambda v, c0=c0, h0=h0: v.reduce_max(
                nm16[:, h0:h0 + 8],
                sm_s[:, c0:c0 + 1024].rearrange("p (a b) -> p a b", b=128),
                axis=AX.X, negate=True), vP)
            nmm = vP.n

            wait("scalar", vP, nmm)
            if b > 0:
                wait("scalar", vP, pbfv[(b - 1, half)])
            def s_exp(s, h0=h0):
                last = None
                for h in range(h0, h0 + 8):
                    last = s.activation(
                        p32_s[:, h * 128:(h + 1) * 128],
                        sm_s[:, h * 128:(h + 1) * 128], AF.Exp,
                        bias=nm16[:, h:h + 1], accum_out=rs16[:, h:h + 1])
                return last
            expv[(b, half)] = run("scalar", s_exp, sP)

        for half in range(2):
            c0 = half * 1024
            h0 = half * 8
            wait("vector", sP, expv[(b, half)])
            run("vector", lambda v, h0=h0: v.reciprocal(
                ri16[:, h0:h0 + 8], rs16[:, h0:h0 + 8]), vP)
            m_rc = vP.n
            wait("vector", vP, m_rc)
            if b > 0:
                wait("vector", tP, ptm[(b - 1, half)])
            pbfv[(b, half)] = run("vector", lambda v, c0=c0, h0=h0: v.tensor_tensor(
                pbf_s[:, c0:c0 + 1024].rearrange("p (a b) -> p a b", b=128),
                p32_s[:, c0:c0 + 1024].rearrange("p (a b) -> p a b", b=128),
                ri16[:, h0:h0 + 8].unsqueeze(2).broadcast_to([128, 8, 128]),
                ALU.mult), vP)

        for half in range(2):
            c0 = half * 1024
            h0 = half * 8
            if half == 1 and b + 1 < NT:
                emit_sc(b + 1)
            wait("tensor", vP, pbfv[(b, half)])
            if b > 0:
                wait("tensor", sP, ptev[(b - 1, half)])
            def t_pt(t, h0=h0):
                last = None
                for h in range(h0, h0 + 8):
                    last = t.transpose(psPT[:, h * 128:(h + 1) * 128],
                                       pbf_s[:, h * 128:(h + 1) * 128],
                                       ident_b[:, :])
                return last
            ptm[(b, half)] = run("tensor", t_pt, tP)

            wait("scalar", tP, ptm[(b, half)])
            if b > 0:
                wait("scalar", tP, pvm[(b - 1, half)])
            ptev[(b, half)] = run("scalar", lambda s, c0=c0: s.copy(
                pT_s[:, c0:c0 + 1024], psPT[:, c0:c0 + 1024]), sP)

            wait("tensor", sP, ptev[(b, half)])
            if b > 0:
                wait("tensor", vP, oev[(b - 1, half)])
            def t_pv(t, b=b, h0=h0):
                last = None
                for h in range(h0, h0 + 8):
                    r0 = 64 * (h % 2)
                    cc = (h // 2) * 128
                    last = t.matmul(psO[r0:r0 + 64, cc:cc + 128],
                                    vh_s[:, b * 1024 + h * 64:
                                         b * 1024 + (h + 1) * 64],
                                    pT_s[:, h * 128:(h + 1) * 128],
                                    start=True, stop=True,
                                    tile_position=(0, r0))
                return last
            pvm[(b, half)] = run("tensor", t_pv, tP)

            wait("vector", tP, pvm[(b, half)])
            def v_oe(v, b=b, half=half):
                last = None
                for m in range(half * 4, half * 4 + 4):
                    last = v.tensor_scalar_add(
                        o_fm[:, m * 1024 + b * 128: m * 1024 + (b + 1) * 128],
                        psO[:, m * 128:(m + 1) * 128], bv_s[:, m:m + 1])
                return last
            oev[(b, half)] = run("vector", v_oe, vP)
    m_p3_t = tP.n
    m_p3_v = vP.n
    if _STOP == "P3":
        return _finish_early(vP, vP.n)

    free("ri16", "rs16", "nm16", "pT_s", "pbf_s", "p32_s", "ss_s", "sm_s",
         "mask16")
    free("psO", "psPT", "psA")
    o2_s = sbuf("o2_s", [128, 8192], BF16, side="right")
    psC2 = [psum("psC2a", [128, 1024], F32), psum("psC2b", [128, 1024], F32)]

    p4ev = {}
    for m in range(8):
        wait("tensor", dW2b, m_wo)
        wait("tensor", vP, p4ev[m - 2] if m >= 2 else m_p3_v)
        def t_p4(t, m=m):
            last = None
            for k in range(8):
                for hf in range(2):
                    last = t.matmul(
                        psC2[m % 2][:, hf * 512:(hf + 1) * 512],
                        wo_s[:, (k * 8 + m) * 128:(k * 8 + m + 1) * 128],
                        o_fm[:, k * 1024 + hf * 512: k * 1024 + (hf + 1) * 512],
                        start=(k == 0), stop=(k == 7))
            return last
        mte = run("tensor", t_p4, tP)
        wait("vector", tP, mte)
        p4ev[m] = run("vector", lambda v, m=m: v.tensor_scalar_add(
            o2_s[:, m * 1024:(m + 1) * 1024], psC2[m % 2][:, :],
            bo_s[:, m:m + 1]), vP)
    m_p4_t = tP.n
    m_p4_v = vP.n
    if _STOP == "P4":
        return _finish_early(vP, m_p4_v)

    free("wv_s") if False else None
    free("o_fm", "vh_s", "kh_s", "qh_s", "wo_s")
    free("psC2b", "psC2a")
    u2tm = sbuf("u2tm", [128, 32768], BF16)
    x2 = [sbuf("x2a", [128, D], F32), sbuf("x2b", [128, D], F32)]
    xq = [sbuf("xqa", [128, 1024], F32), sbuf("xqb", [128, 1024], F32)]
    psP = psum("psP", [128, 1024], F32)

    xq_m = {}
    tt_m = {}
    st5_m = {}
    nrm5_m = {}
    x2dma_m = {}
    def emit_xq(t, qb):
        if t == 0 and qb == 0:
            wait("gpsimd", tP, m_p4_t)
        if qb >= 2:
            wait("gpsimd", vP, tt_m[(t, (qb - 2) * 2 + 1)])
        elif t >= 1:
            wait("gpsimd", vP, tt_m[(t - 1, (qb + 2) * 2 + 1)])
        run("gpsimd", lambda g, t=t, qb=qb: g.dma_start(
            xq[qb % 2][:, :],
            x_d[t * 128:(t + 1) * 128, qb * 1024:(qb + 1) * 1024]),
            dX5[qb % 2], 16)
        xq_m[(t, qb)] = dX5[qb % 2].n

    st5a_m = {}

    def emit_ln5a(t):
        wait("scalar", vP, tt_m[(t, 7)])
        st5_m[t] = run("scalar", lambda s, t=t: s.activation(
            u2tm[:, t * 4096:(t + 1) * 4096], x2[t % 2][:, :],
            AF.Square, accum_out=st2[:, 1:2]), sP)
        vs5_m[t] = run("vector", lambda v, t=t: v.tensor_scalar(
            trash2[:, :], x2[t % 2][:, :], 1.0, 0.0,
            ALU.mult, ALU.add, accum_out=st2[:, 0:1]), vP)

    def emit_ln5b(t):
        wait("vector", sP, st5_m[t])
        wait("scalar", vP, vs5_m[t])
        run("scalar", lambda s: s.activation(
            st2[:, 2:3], st2[:, 0:1], AF.Identity, scale=cst[:, 3:4]), sP)
        run("scalar", lambda s: s.activation(
            st2[:, 0:1], st2[:, 2:3], AF.Square), sP)
        mq = run("scalar", lambda s: s.activation(
            st2[:, 1:2], st2[:, 1:2], AF.Identity, scale=cst[:, 3:4],
            bias=cst[:, 0:1]), sP)
        wait("vector", sP, mq)
        m4 = run("vector", lambda v: v.tensor_sub(
            st2[:, 3:4], st2[:, 1:2], st2[:, 0:1]), vP)
        wait("scalar", vP, m4)
        run("scalar", lambda s: s.activation(
            rst[:, :], st2[:, 3:4], AF.Ln), sP)
        mr5 = run("scalar", lambda s: s.activation(
            rst[:, :], rst[:, :], AF.Exp, scale=cst[:, 1:2]), sP)
        wait("vector", sP, mr5)
        nrm5_m[t] = run("vector", lambda v, t=t: v.tensor_scalar(
            u2tm[:, t * 4096:(t + 1) * 4096], x2[t % 2][:, :],
            st2[:, 2:3], rst[:, 0:1], ALU.subtract, ALU.mult), vP)

    for t in range(NT):
        emit_xq(t, 0)
        emit_xq(t, 1)
        for bk in range(8):
            if bk == 0 and t >= 1:
                emit_ln5a(t - 1)
            if bk == 2 and t >= 1:
                emit_ln5b(t - 1)
            if bk == 4:
                emit_xq(t, 2)
            if bk == 6:
                emit_xq(t, 3)
            gi = t * 8 + bk
            wait("tensor", dW3, m_pw)
            wait("tensor", dWb, m_pb)
            if gi >= 2:
                wait("tensor", vP, tt_m[(t, bk - 2)] if bk >= 2
                     else tt_m[(t - 1, 6 + bk)])
            else:
                wait("tensor", vP, m_p4_v)
            if bk == 0 and t >= 2:
                wait("tensor", sP, st5_m[t - 2])
                wait("tensor", vP, nrm5_m[t - 2])
                wait("tensor", dO[t % 2], x2dma_m[t - 2])
            def t_p5(t_, t=t, bk=bk):
                par = bk % 2
                for r in range(8):
                    t_.matmul(psP[:, par * 512:(par + 1) * 512],
                              o2_s[:, r * 1024 + t * 128:
                                   r * 1024 + (t + 1) * 128],
                              pw_s[:, r * 4096 + bk * 512:
                                   r * 4096 + (bk + 1) * 512],
                              start=(r == 0), stop=False)
                return t_.matmul(psP[:, par * 512:(par + 1) * 512],
                                 onesT[32:33, 0:128],
                                 LBP[32:33, bk * 512:(bk + 1) * 512],
                                 start=False, stop=True)
            mte = run("tensor", t_p5, tP)
            wait("vector", tP, mte)
            wait("vector", dX5[(bk // 2) % 2], xq_m[(t, bk // 2)])
            if t >= 2 and bk == 0:
                wait("vector", sP, st5_m[t - 2])
                wait("vector", dO[t % 2], x2dma_m[t - 2])
            tt_m[(t, bk)] = run("vector", lambda v, t=t, bk=bk: v.tensor_tensor(
                x2[t % 2][:, bk * 512:(bk + 1) * 512],
                psP[:, (bk % 2) * 512:((bk % 2) + 1) * 512],
                xq[(bk // 2) % 2][:, (bk % 2) * 512:((bk % 2) + 1) * 512],
                ALU.add), vP)
            if bk % 2 == 1:
                qq = bk // 2
                wait("sync", vP, tt_m[(t, bk)])
                run("sync", lambda q, t=t, qq=qq: q.dma_start(
                    out_d[t * 128:(t + 1) * 128, qq * 1024:(qq + 1) * 1024],
                    x2[t % 2][:, qq * 1024:(qq + 1) * 1024]), dO[t % 2], 16)

        x2dma_m[t] = dO[t % 2].n
    emit_ln5a(NT - 1)
    emit_ln5b(NT - 1)
    dO_all = (dO[0].n, dO[1].n)
    m_p5_t = tP.n
    if _STOP == "P5":
        return _finish_early(vP, nrm5_m[NT - 1])

    free("xqb", "xqa", "x2b", "x2a")
    free("o2_s", "pw_s")
    free("psP")
    u2fm = sbuf("u2fm", [128, 32768], BF16, side="right")
    la2 = sbuf("la2", [128, 4096], BF16, side="right")
    lball = sbuf("lball", [128, 4096], BF16, side="right")
    rwt = sbuf("rwt", [128, 512], BF16, side="right")
    wait("gpsimd", tP, m_p5_t)
    run("gpsimd", lambda g: g.dma_start(la2[:, :], la_d), dWa, 16)
    run("gpsimd", lambda g: g.dma_start(lball[:, :], lb_d), dWa, 16)
    run("gpsimd", lambda g: g.dma_start(rwt[:, :], rw_d), dWa, 16)
    m_lw2 = dWa.n
    psT2 = [psum("psT2a", [128, 1024], BF16), psum("psT2b", [128, 1024], BF16)]

    g2 = 0
    tev2 = {}
    for t in range(NT):
        for grp in range(4):
            wait("tensor", vP, nrm5_m[t])
            if g2 < 2:
                wait("tensor", vP, tt_m[(NT - 1, 7)])
            if g2 >= 2:
                wait("tensor", vP, tev2[g2 - 2])
            def t_g2(tt, t=t, grp=grp, par=g2 % 2):
                last = None
                for j in range(8):
                    hcc = grp * 8 + j
                    last = tt.transpose(psT2[par][:, j * 128:(j + 1) * 128],
                                        u2tm[:, t * 4096 + hcc * 128:
                                             t * 4096 + (hcc + 1) * 128],
                                        ident_b[:, :])
                return last
            mT = run("tensor", t_g2, tP)
            wait("vector", tP, mT)
            def v_g2(v, t=t, grp=grp, par=g2 % 2):
                o = bass.AP(u2fm, (grp * 8) * 1024 + t * 128,
                            [[32768, 128], [1024, 8], [1, 128]])
                i = bass.AP(psT2[par], 0, [[1024, 128], [128, 8], [1, 128]])
                return v.tensor_copy(o, i)
            tev2[g2] = run("vector", v_g2, vP)
            g2 += 1
    m_p5b_v = vP.n
    if _STOP == "P5b":
        return _finish_early(vP, m_p5b_v)

    free("u2tm")
    free("psT2b", "psT2a")
    lgf = sbuf("lgf", [4, 1024], F32)
    lga = sbuf("lga", [16, 1024], BF16)
    alf = sbuf("alf", [4, 1024], F32)
    alfb = sbuf("alfb", [4, 1024], BF16)
    zscB = sbuf("zscB", [128, 1024], BF16)
    ua4 = sbuf("ua4", [128, 4], F32)
    al4 = sbuf("al4", [128, 4], F32)
    nm4 = sbuf("nm4", [128, 1], F32)
    rc4 = sbuf("rc4", [128, 1], F32)
    ae_s = sbuf("ae_s", [128, 1024], F32)
    sel16b = sbuf("sel16b", [4, 128], BF16)
    psZ = psum("psZ", [128, 1024], F32)
    psL = psum("psL", [16, 1024], F32)
    psM = psum("psM", [128, 512], F32)
    psE = psum("psE", [128, 1024], F32)

    wait("tensor", vP, m_p5b_v)
    wait("tensor", dWa, m_lw2)
    def t_z(t):
        last = None
        for hcc in range(HC):
            for hf in range(2):
                last = t.matmul(psZ[:, hf * 512:(hf + 1) * 512],
                                la2[:, hcc * 128:(hcc + 1) * 128],
                                u2fm[:, hcc * 1024 + hf * 512:
                                     hcc * 1024 + (hf + 1) * 512],
                                start=(hcc == 0), stop=(hcc == HC - 1))
        return last
    m_z = run("tensor", t_z, tP)

    def t_lg(t):
        last = None
        for hcc in range(HC):
            for hf in range(2):
                last = t.matmul(psL[:, hf * 512:(hf + 1) * 512],
                                rwt[:, hcc * 16:(hcc + 1) * 16],
                                u2fm[:, hcc * 1024 + hf * 512:
                                     hcc * 1024 + (hf + 1) * 512],
                                start=(hcc == 0), stop=(hcc == HC - 1))
        return last
    m_lg = run("tensor", t_lg, tP)
    wait("vector", tP, m_lg)
    wait("vector", dO[0], dO_all[0])
    wait("vector", dO[1], dO_all[1])
    m_lga = run("vector", lambda v: v.tensor_copy(lga[:, :], psL[:, :]), vP)
    wait("vector", vP, m_lga)
    m_ohb2 = run("vector", lambda v: v.tensor_copy(sel16b[:, :], sel16[:, :]),
                 vP)
    wait("tensor", vP, m_ohb2)
    def t_sel(t):
        t.matmul(psM[0:4, 0:512], ohb[:, :], lga[:, 0:512],
                 start=True, stop=True)
        return t.matmul(psM[0:4, 0:512][0:4, 0:512] if False else
                        psL[0:4, 0:512], ohb[:, :], lga[:, 512:1024],
                        start=True, stop=True)
    m_self = run("tensor", t_sel, tP)
    wait("vector", tP, m_self)
    def v_self(v):
        v.tensor_copy(lgf[:, 0:512], psM[0:4, 0:512])
        return v.tensor_copy(lgf[:, 512:1024], psL[0:4, 0:512])
    m_lgf = run("vector", v_self, vP)

    alf_m = {}
    for t in range(NT):
        wait("tensor", vP, m_lgf if t == 0 else alf_m[t - 1])
        m_t1 = run("tensor", lambda tt, t=t: tt.transpose(
            psM[:, 0:4], lgf[:, t * 128:(t + 1) * 128], ident_f[0:4, 0:4]), tP)
        wait("vector", tP, m_t1)
        m_ua = run("vector", lambda v: v.tensor_add(
            ua4[:, :], psM[:, 0:4], ctile[:, :]), vP)
        wait("vector", vP, m_ua)
        m_n4 = run("vector", lambda v: v.reduce_max(
            nm4[:, :], ua4[:, :], axis=AX.X, negate=True), vP)
        wait("scalar", vP, m_n4)
        m_e4 = run("scalar", lambda s: s.activation(
            al4[:, :], ua4[:, :], AF.Exp, bias=nm4[:, 0:1]), sP)
        wait("vector", sP, m_e4)
        m_s4 = run("vector", lambda v: v.reduce_sum(
            nm4[:, :], al4[:, :], axis=AX.X), vP)
        wait("vector", vP, m_s4)
        m_r4 = run("vector", lambda v: v.reciprocal(rc4[:, :], nm4[:, :]), vP)
        wait("vector", vP, m_r4)
        m_a4 = run("vector", lambda v: v.tensor_scalar_mul(
            ua4[:, :], al4[:, :], rc4[:, 0:1]), vP)
        wait("tensor", vP, m_a4)
        m_t2 = run("tensor", lambda tt: tt.transpose(
            psM[0:4, 4:132], ua4[:, :], ident_f[:, :]), tP)
        wait("vector", tP, m_t2)
        alf_m[t] = run("vector", lambda v, t=t: v.tensor_copy(
            alfb[:, t * 128:(t + 1) * 128], psM[0:4, 4:132]), vP)

    wait("tensor", vP, alf_m[NT - 1])
    def t_ae(t):
        t.matmul(psE[:, 0:512], sel16b[:, :], alfb[:, 0:512],
                 start=True, stop=True)
        return t.matmul(psE[:, 512:1024], sel16b[:, :], alfb[:, 512:1024],
                        start=True, stop=True)
    m_ae = run("tensor", t_ae, tP)
    wait("vector", tP, m_ae)
    m_aes = run("vector", lambda v: v.tensor_copy(ae_s[:, :], psE[:, :]), vP)
    wait("vector", vP, m_aes)
    wait("vector", tP, m_z)
    m_zsc = run("vector", lambda v: v.scalar_tensor_tensor(
        zscB[:, :], psZ[:, :], 2.0, ae_s[:, :], ALU.mult, ALU.mult), vP)

    free("sel16b", "ae_s", "rc4", "nm4", "al4", "ua4", "zscB", "alfb", "alf", "lga", "lgf")
    wbb = [sbuf("wbb%d" % i, [128, 4096], BF16, side="right") for i in range(4)]
    ff_fm = [sbuf("ffa", [128, 1024], BF16), sbuf("ffb", [128, 1024], BF16)]
    ff_t = [sbuf("fta", [128, 1024], F32), sbuf("ftb", [128, 1024], F32)]

    wb_m = {}
    wait("gpsimd", tP, m_p5_t)
    wait("gpsimd", vP, nrm5_m[NT - 1])
    wait("gpsimd", dO[0], dO_all[0])
    wait("gpsimd", dO[1], dO_all[1])
    for oc in range(4):
        run("gpsimd", lambda g, oc=oc: g.dma_start(
            wbb[oc % 4][:, :], wb_d[oc]), dW4[oc % 4], 16)
        wb_m[oc] = dW4[oc % 4].n

    free("psE", "psM", "psL", "psZ")
    psF = [psum("psFa", [128, 1024], F32), psum("psFb", [128, 1024], F32)]
    psX = [psum("psXa", [128, 1024], BF16), psum("psXb", [128, 1024], BF16)]

    f_mm = {}; f_ev1 = {}; f_tt = {}; f_ev2 = {}; f_dma = {}
    for oc in range(32):
        wait("tensor", dW4[oc % 4], wb_m[oc])
        wait("tensor", vP, f_ev1[oc - 2] if oc >= 2 else m_zsc)
        if oc == 0:
            wait("tensor", dWb, m_pb)
            wait("vector", dO[0], dO_all[0])
            wait("vector", dO[1], dO_all[1])
        def t_f(t, oc=oc):
            par = oc % 2
            for hcc in range(HC):
                for hf in range(2):
                    t.matmul(psF[par][:, hf * 512:(hf + 1) * 512],
                             wbb[oc % 4][:, hcc * 128:(hcc + 1) * 128],
                             u2fm[:, hcc * 1024 + hf * 512:
                                  hcc * 1024 + (hf + 1) * 512],
                             start=(hcc == 0), stop=False)
            last = None
            for hf in range(2):
                last = t.matmul(psF[par][:, hf * 512:(hf + 1) * 512],
                                lball[:, oc * 128:(oc + 1) * 128],
                                zscB[:, hf * 512:(hf + 1) * 512],
                                start=False, stop=True)
            return last
        f_mm[oc] = run("tensor", t_f, tP)
        if oc + 4 < 32:
            wait("gpsimd", tP, f_mm[oc])
            run("gpsimd", lambda g, oc=oc: g.dma_start(
                wbb[(oc + 4) % 4][:, :], wb_d[oc + 4]), dW4[(oc + 4) % 4], 16)
            wb_m[oc + 4] = dW4[(oc + 4) % 4].n
        wait("vector", tP, f_mm[oc])
        f_ev1[oc] = run("vector", lambda v, oc=oc: v.tensor_scalar_add(
            ff_fm[oc % 2][:, :], psF[oc % 2][:, :], bb_s[:, oc:oc + 1]), vP)
        wait("tensor", vP, f_ev1[oc])
        if oc >= 2:
            wait("tensor", vP, f_ev2[oc - 2])
        def t_ft(t, oc=oc):
            last = None
            for j in range(8):
                last = t.transpose(psX[oc % 2][:, j * 128:(j + 1) * 128],
                                   ff_fm[oc % 2][:, j * 128:(j + 1) * 128],
                                   ident_b[:, :])
            return last
        f_tt[oc] = run("tensor", t_ft, tP)
        wait("vector", tP, f_tt[oc])
        if oc >= 2:
            wait("vector", dA[oc % 2], f_dma[oc - 2])
        f_ev2[oc] = run("vector", lambda v, oc=oc: v.tensor_copy(
            ff_t[oc % 2][:, :], psX[oc % 2][:, :]), vP)
        wait("gpsimd", vP, f_ev2[oc])
        if oc == 0:
            wait("gpsimd", dO[0], dO_all[0])
            wait("gpsimd", dO[1], dO_all[1])
        def g_acc(g, oc=oc):
            o = bass.AP(out_h, oc * 128,
                        [[4096, 128], [128 * 4096, 8], [1, 128]])
            if _ACCUM:
                return g.dma_start(o, ff_t[oc % 2][:, :], accum_op=ALU.add)
            return g.dma_start(o, ff_t[oc % 2][:, :])
        run("gpsimd", g_acc, dA[oc % 2], 16)
        f_dma[oc] = dA[oc % 2].n

    wait("gpsimd", dO[0], dO_all[0])
    wait("gpsimd", dO[1], dO_all[1])
    wait("gpsimd", dA[0], dA[0].n)
    wait("gpsimd", dA[1], dA[1].n)

    block = es.enter_context(nc.Block())
    for eng in ENGS:
        def _runner(e, fns=tuple(prog[eng])):
            for f in fns:
                f(e)
        getattr(block, eng)(_runner)

    for name in reversed(list(scopes)):
        scopes.pop(name).close()
    es.close()
    return nc


def _prep(inputs):
    f = np.ascontiguousarray
    x = inputs["L0_attn_inputs"]
    conv_w = inputs["L0_attn_conv_w"]
    conv_b = inputs["L0_attn_conv_b"]
    wq = inputs["L0_attn_wq"].reshape(1024, 1024)
    bq = inputs["L0_attn_bq"].reshape(1024)
    wk = inputs["L0_attn_wk"].reshape(1024, 1024)
    bk = inputs["L0_attn_bk"].reshape(1024)
    wv = inputs["L0_attn_wv"].reshape(1024, 1024)
    bv = inputs["L0_attn_bv"].reshape(1024)
    wo = inputs["L0_attn_wo"].reshape(1024, 1024)
    bo = inputs["L0_attn_bo"].reshape(1024)
    pw = inputs["L0_attn_proj_w"]
    pb = inputs["L0_attn_proj_b"]
    wb = inputs["L0_ffn_W_base"]
    bb = inputs["L0_ffn_b_base"]
    la = inputs["L0_ffn_lora_A"]
    lb = inputs["L0_ffn_lora_B"]
    rw = inputs["L0_ffn_router_W"]
    rb = inputs["L0_ffn_router_b"]
    pr = inputs["L0_ffn_domain_prior"]
    ac = inputs["L0_ffn_active"]

    com = {
        "cw": f(np.asarray(conv_w).reshape(3, 32, 128, 24, 128)
                .transpose(3, 2, 0, 1, 4).reshape(24, 128, 12288)),
        "cb": f(np.asarray(conv_b).reshape(24, 128).T),
        "wq": f(np.asarray(wq).reshape(8, 128, 8, 128)
                .transpose(1, 0, 2, 3).reshape(128, 8192)),
        "wk": f(np.asarray(wk).reshape(8, 128, 8, 128)
                .transpose(1, 0, 2, 3).reshape(128, 8192)),
        "wv": f(np.asarray(wv).reshape(8, 128, 1024)
                .transpose(1, 0, 2).reshape(128, 8192)),
        "wo": f(np.asarray(wo).reshape(8, 128, 8, 128)
                .transpose(1, 0, 2, 3).reshape(128, 8192)),
        "bq": f(np.asarray(bq).reshape(8, 128).T),
        "bk": f(np.asarray(bk).reshape(8, 128).T),
        "bo": f(np.asarray(bo).reshape(8, 128).T),
        "bv": f(np.asarray(bv).reshape(8, 128).T),
        "pw": f(np.asarray(pw).reshape(8, 128, 4096)
                .transpose(1, 0, 2).reshape(128, 32768)),
        "pb": f(np.asarray(pb).reshape(1, 4096)),
        "wb": f(np.asarray(wb).T.reshape(32, 128, 32, 128)
                .transpose(2, 1, 0, 3).reshape(32, 128, 4096)),
        "bb": f(np.asarray(bb).reshape(32, 128).T),
        "la": f(np.asarray(la).transpose(2, 0, 1).reshape(4096, 128)
                .reshape(32, 128, 128).transpose(1, 0, 2).reshape(128, 4096)),
        "lb": f(np.asarray(lb).transpose(0, 2, 1).reshape(128, 4096)),
        "rw": f(np.asarray(rw).T.reshape(32, 128, 16)
                .transpose(1, 0, 2).reshape(128, 512)),
        "rb": f(np.asarray(rb).reshape(16, 1)),
        "pr": f(np.asarray(pr).reshape(16, 1)),
        "ac": f(np.asarray(ac).reshape(1, 4)),
    }

    x = np.asarray(x)
    in_maps = []
    for c in range(8):
        lo, hi = c * 512, (c + 1) * 512
        xs = f(np.concatenate([x[0, lo:hi, :], x[1, lo:hi, :]], axis=0))
        halo = np.zeros((4, 4096), dtype=np.float32)
        if lo > 0:
            halo[0] = x[0, lo - 1, :]
            halo[2] = x[1, lo - 1, :]
        if hi < 4096:
            halo[1] = x[0, hi, :]
            halo[3] = x[1, hi, :]
        m = {"x": xs, "xh": f(halo)}
        m.update(com)
        in_maps.append(m)
    return in_maps


def kernel(**inputs):
    in_maps = _prep(inputs)
    nc = build_program()
    res = run_bass_kernel_spmd(nc, in_maps, core_ids=list(range(8)))
    out = np.empty((2, 4096, 4096), dtype=np.float32)
    for c in range(8):
        r = res.results[c]["out"]
        out[0, c * 512:(c + 1) * 512, :] = r[0:512]
        out[1, c * 512:(c + 1) * 512, :] = r[512:1024]
    return out
